# version 27 (final)
# baseline (speedup 1.0000x reference)
from contextlib import ExitStack

import numpy as np

import concourse.bass as bass
import concourse.mybir as mybir
from concourse.bass_utils import run_bass_kernel_spmd
from concourse.aot_env import allow_ici_all_to_all

F32 = mybir.dt.float32
BF16 = mybir.dt.bfloat16
F8 = mybir.dt.float8e4
AT = mybir.AluOpType
AF = mybir.ActivationFunctionType

S = 8192
D = 2048
NH = 16
HD = 128
FF = 6144
NCORES = 8
TPC = S // NCORES
NDC = D // 128
NIC = FF // 128
EPS = 1e-5
SCALE = float(np.float32(1.0) / np.sqrt(np.float32(128.0)))

SEC_DN = 2 * D * 768
PF_N = SEC_DN + FF * 256

SEC = 256 * 1024
KV_PER = 4 * SEC
AO_PER = 2 * SEC

DBG = ()


class Rec:

    ENGS = ("T", "V", "S", "G", "Y")

    NSEM = 8

    def __init__(self):
        self.ops = {e: [] for e in self.ENGS}
        self.cnt = {"T": 0, "V": 0, "S": 0, "G": 0, "Y": 0, "cc": 0}
        self.step = {"T": 1, "V": 1, "S": 1, "G": 1, "Y": 1, "cc": 1}
        for q in ("qS", "qG", "qA"):
            self.cnt[q] = 0
            for i in range(self.NSEM):
                self.cnt[f"{q}{i}"] = 0
                self.step[f"{q}{i}"] = 16

    def qbar(self, queue):
        return [(f"{queue}{i}", self.cnt[f"{queue}{i}"] * 16)
                for i in range(self.NSEM) if self.cnt[f"{queue}{i}"] > 0]

    def op(self, eng, fn, waits=(), mark=True, ctr=None):
        ctr = ctr or eng
        flat = []
        for w in waits:
            if w is None:
                continue
            if isinstance(w, list):
                flat.extend(x for x in w if x is not None)
            else:
                flat.append(w)
        waits = flat
        if eng in ("V", "S", "G") and self.cnt[eng] > 0:
            waits.append((eng, self.cnt[eng] * self.step[eng]))
        for (cn, cv) in waits:
            assert cv <= self.cnt[cn] * self.step[cn], (
                f"wait on future token {cn}={cv} (now {self.cnt[cn] * self.step[cn]})")
        tok = None
        if mark:
            self.cnt[ctr] += 1
            tok = (ctr, self.cnt[ctr] * self.step[ctr])
        self.ops[eng].append((waits, fn, ctr if mark else None))
        return tok

    def dma(self, eng, queue, out, in_, waits=()):
        i = self.cnt[queue]
        self.cnt[queue] += 1
        sem = f"{queue}{i % self.NSEM}"
        w = [x for x in waits if x is not None]
        if i >= self.NSEM:
            w.append((sem, 16 * (i // self.NSEM)))

        def fn(e, out=out, in_=in_):
            return e.dma_start(out=out, in_=in_)
        return self.op(eng, fn, waits=w, ctr=sem)

    def emit_block(self, nc, sems):
        with nc.Block() as blk:
            def mk(ename):
                ops = self.ops[ename]

                def body(eh):
                    wm = {}
                    for waits, fn, ctr in ops:
                        for (cn, cv) in waits:
                            if wm.get(cn, 0) < cv:
                                eh.wait_ge(sems[cn], cv)
                                wm[cn] = cv
                        instr = fn(eh)
                        if ctr is not None:
                            instr.then_inc(sems[ctr], self.step[ctr])
                return body

            for ename, reg in (("T", blk.tensor), ("V", blk.vector),
                               ("S", blk.scalar), ("G", blk.gpsimd),
                               ("Y", blk.sync)):
                if self.ops[ename]:
                    reg(mk(ename))
        self.ops = {e: [] for e in self.ENGS}


def dap(t, off, pattern):
    return bass.AP(t, off, pattern)


def build_program(dbg=()):
    with allow_ici_all_to_all():
        return _build_program(dbg)


def _build_program(dbg=()):
    nc = bass.Bass("TRN2", target_bir_lowering=False)

    x_in = nc.dram_tensor("x", [TPC, D], F32, kind="ExternalInput")
    wqkvo = nc.dram_tensor("wqkvo", [4, D, D], F32, kind="ExternalInput")
    gu_in = nc.dram_tensor("gu", [2, D, 768], F32, kind="ExternalInput")
    dn_in = nc.dram_tensor("dn", [FF, 256], F32, kind="ExternalInput")
    gwt_in = nc.dram_tensor("gwt", [HD, HD], F32, kind="ExternalInput")
    gb_in = nc.dram_tensor("gb", [HD, 1], F32, kind="ExternalInput")
    gs_in = nc.dram_tensor("gs", [128, NIC], F32, kind="ExternalInput")
    us_in = nc.dram_tensor("us", [128, NIC], F32, kind="ExternalInput")
    ds_in = nc.dram_tensor("ds", [1, D], F32, kind="ExternalInput")
    id_in = nc.dram_tensor("ident", [128, 128], F32, kind="ExternalInput")
    mask_in = nc.dram_tensor("masks", [4, 128, 512], F32, kind="ExternalInput")
    out_t = nc.dram_tensor("out", [TPC, D], F32, kind="ExternalOutput")

    pf_in = nc.dram_tensor("pf_in", [PF_N], F8)
    pf_out = nc.dram_tensor("pf_out", [NCORES * PF_N], F8)
    kv_in = nc.dram_tensor("kv_in", [NCORES * KV_PER], BF16)
    kv_out = nc.dram_tensor("kv_out", [NCORES * KV_PER], BF16)
    Q_PER = 2 * SEC
    q_in = nc.dram_tensor("q_in", [NCORES * Q_PER], BF16)
    q_out = nc.dram_tensor("q_out", [NCORES * Q_PER], BF16)
    AO2 = 2 * 256 * 512
    ao_inA = nc.dram_tensor("ao_inA", [NCORES * AO2], BF16)
    ao_outA = nc.dram_tensor("ao_outA", [NCORES * AO2], BF16)
    ao_inB = nc.dram_tensor("ao_inB", [NCORES * AO2], BF16)
    ao_outB = nc.dram_tensor("ao_outB", [NCORES * AO2], BF16)
    x2_sp = nc.dram_tensor("x2_sp", [TPC, D], F32)

    dbg_taps = []
    if "kvq" in dbg:
        dbg_taps.append((nc.dram_tensor("dbg_kv", [NCORES * KV_PER], BF16,
                                        kind="ExternalOutput"), kv_out))
        pass
    if "prep" in dbg:
        pass
        dbg_taps.append((nc.dram_tensor("dbg_prepf", [NCORES * PF_N], F8,
                                        kind="ExternalOutput"), pf_out))

    if "pre" in dbg:
        dbg_taps.append((nc.dram_tensor("dbg_kvin", [NCORES * KV_PER], BF16,
                                        kind="ExternalOutput"), kv_in))
        dbg_taps.append((nc.dram_tensor("dbg_qin", [NCORES * Q_PER], BF16,
                                        kind="ExternalOutput"), q_in))
    dbg_xnt = None
    if "xnt" in dbg:
        dbg_xnt = (nc.dram_tensor("dbg_xnth", [128, NDC * TPC], BF16,
                                  kind="ExternalOutput"),
                   nc.dram_tensor("dbg_xntl", [128, NDC * TPC], BF16,
                                  kind="ExternalOutput"))
    if "x2" in dbg:
        dbg_taps.append((nc.dram_tensor("dbg_x2", [TPC, D], F32,
                                        kind="ExternalOutput"), x2_sp))

    with ExitStack() as st:
        sems = {}
        sem_names = ["T", "V", "S", "G", "Y", "cc"]
        for q in ("qS", "qG", "qA"):
            sem_names += [f"{q}{i}" for i in range(Rec.NSEM)]
        for name in sem_names:
            sems[name] = st.enter_context(nc.semaphore("sem_" + name))

        pb = [st.enter_context(nc.psum_tensor(f"pb{i}", [128, 512], F32))
              for i in range(8)]

        def SB(name, shape, dt=F32, stk=None):
            return (stk or st).enter_context(nc.sbuf_tensor(name, shape, dt))

        sident = SB("sident", [128, 128])
        sgwt32 = SB("sgwt32", [128, 128])
        sgwt_h = SB("sgwt_h", [128, 128], BF16)
        sgwt_l = SB("sgwt_l", [128, 128], BF16)
        sgb = SB("sgb", [128, 1])
        sgs = SB("sgs", [128, NIC])
        sus = SB("sus", [128, NIC])
        sones_b = SB("sones_b", [128, 1], BF16)
        sones_f = SB("sones_f", [1, 128])

        xnT_h = SB("xnT_h", [128, NDC * TPC], BF16)
        xnT_l = SB("xnT_l", [128, NDC * TPC], BF16)
        r = Rec()
        stBC = ExitStack()
        owh = SB("owh", [128, NDC * 512], BF16, stk=stBC)
        owl = SB("owl", [128, NDC * 512], BF16, stk=stBC)

        stA = ExitStack()
        stAB = ExitStack()

        c_id = r.dma("Y", "qS", sident[:, :], id_in.ap())
        c_gb = r.dma("Y", "qS", sgb[:, :], gb_in.ap())
        c_gs = r.dma("Y", "qS", sgs[:, :], gs_in.ap())
        c_us = r.dma("Y", "qS", sus[:, :], us_in.ap())
        c_gwt = r.dma("Y", "qS", sgwt32[:, :], gwt_in.ap())
        t_ones_b = r.op("G", lambda g: g.memset(sones_b[:, :], 1.0))
        t_ones_f = r.op("G", lambda g: g.memset(sones_f[:, :], 1.0))

        t_gwh = r.op("V", lambda v: v.tensor_copy(sgwt_h[:, :], sgwt32[:, :]),
                     waits=[c_gwt])
        t_gwl = r.op("V", lambda v: v.tensor_tensor(sgwt_l[:, :], sgwt32[:, :],
                                                    sgwt_h[:, :], AT.subtract),
                     waits=[t_gwh])
        bank_owner = {j: None for j in range(8)}

        t_c = SB("t_c", [128, 768], stk=stAB); t_d = SB("t_d", [128, 768], stk=stAB)
        t_h = SB("t_h", [128, 768], stk=stAB)
        t_m = SB("t_m", [128, 768], mybir.dt.uint8, stk=stAB)
        qwbuf = [SB(f"qwbuf{i}", [128, 768], stk=stAB) for i in range(2)]
        qobuf = [SB(f"qobuf{i}", [128, 768], F8, stk=stAB) for i in range(2)]
        stLN = ExitStack()


        quant_jobs = []
        for m in range(2):
            for dc in range(NDC):
                quant_jobs.append(("quant",
                                   dap(gu_in, (m * D + dc * 128) * 768, [[768, 128], [1, 768]]),
                                   768,
                                   m * D * 768 + dc * 128 * 768, None, 768))
        for icc in range(NIC):
            quant_jobs.append(("quant",
                               dap(dn_in, icc * 128 * 256, [[256, 128], [1, 256]]),
                               256, SEC_DN + icc * 128 * 256, None, 256))

        slot_free = [None, None]
        prep_last = None

        prep_marks = []

        def run_prep_jobs(jobs, tgt, ldeng="Y", ldq="qS", gen=False,
                          wbuf=None, obuf_h=None, obuf_l=None):
          nonlocal prep_last
          for j, (kind, src, wid, off_h, off_l, rstr) in enumerate(jobs):
            prep_in = tgt
            sl = j % 2
            ld = r.dma(ldeng, ldq, wbuf[sl][:, 0:wid], src, waits=[slot_free[sl]])
            if gen:
                yield
            if kind == "split":
                th = r.op("S", lambda s_, sl=sl, wid=wid: s_.copy(
                    obuf_h[sl][:, 0:wid], wbuf[sl][:, 0:wid]), waits=[ld])
                tl = r.op("V", lambda v, sl=sl, wid=wid: v.tensor_tensor(
                    obuf_l[sl][:, 0:wid], wbuf[sl][:, 0:wid],
                    obuf_h[sl][:, 0:wid], AT.subtract), waits=[th])
                r.dma("G", "qG", dap(prep_in, off_h, [[rstr, 128], [1, wid]]),
                      obuf_h[sl][:, 0:wid], waits=[th])
                s2 = r.dma("G", "qG", dap(prep_in, off_l, [[rstr, 128], [1, wid]]),
                           obuf_l[sl][:, 0:wid], waits=[tl])
                slot_free[sl] = s2
                prep_last = s2
                prep_marks.append(s2)
            else:
                k1 = r.op("V", lambda v, sl=sl, wid=wid: v.tensor_scalar(
                    t_c[:, 0:wid], wbuf[sl][:, 0:wid], 1048577.0, None, AT.mult),
                    waits=[ld])
                if gen:
                    yield
                k2 = r.op("V", lambda v, sl=sl, wid=wid: v.tensor_tensor(
                    t_d[:, 0:wid], t_c[:, 0:wid], wbuf[sl][:, 0:wid], AT.subtract),
                    waits=[k1])
                if gen:
                    yield
                k3 = r.op("V", lambda v, wid=wid: v.tensor_tensor(
                    t_h[:, 0:wid], t_c[:, 0:wid], t_d[:, 0:wid], AT.subtract),
                    waits=[k2])
                if gen:
                    yield
                k4 = r.op("V", lambda v, sl=sl, wid=wid: v.tensor_scalar(
                    t_c[:, 0:wid], wbuf[sl][:, 0:wid], 24576.0, -24576.0,
                    AT.add, AT.add), waits=[ld])
                if gen:
                    yield
                k5a = r.op("V", lambda v, sl=sl, wid=wid: v.tensor_tensor(
                    t_d[:, 0:wid], wbuf[sl][:, 0:wid], wbuf[sl][:, 0:wid],
                    AT.mult), waits=[ld, k3])
                if gen:
                    yield
                k5 = r.op("V", lambda v, wid=wid: v.tensor_scalar(
                    t_m[:, 0:wid], t_d[:, 0:wid], 2.0 ** -12, None,
                    AT.is_ge), waits=[k5a])
                if gen:
                    yield
                k6 = r.op("V", lambda v, sl=sl, wid=wid: v.select(
                    obuf_h[sl][:, 0:wid], t_m[:, 0:wid], t_h[:, 0:wid],
                    t_c[:, 0:wid], add_drain=True), waits=[k3, k4, k5])
                if gen:
                    yield
                s1 = r.dma("G", "qG", dap(prep_in, off_h, [[rstr, 128], [1, wid]]),
                           obuf_h[sl][:, 0:wid], waits=[k6])
                slot_free[sl] = s1
                prep_last = s1



        xtile = [SB(f"xtile{i}", [128, D], stk=stLN) for i in range(2)]
        bnst = SB("bnst", [128, 24], stk=stLN)
        bnag = SB("bnag", [128, 2], stk=stLN)
        lnt = SB("lnt", [128, 1], stk=stLN); lnt2 = SB("lnt2", [128, 1], stk=stLN)
        lnr = SB("lnr", [128, 1], stk=stLN)
        xn = [SB(f"xn{i}", [128, D], stk=stLN) for i in range(2)]

        def ln_chain(r, x_ap, bnst_t, bnag_t, lnt_t, lnt2_t, lnr_t, ld_tok):
            k = None
            for b in range(4):
                k = r.op("V", lambda v, b=b: v.bn_stats(
                    bnst_t[:, b * 6:(b + 1) * 6], x_ap[1](b)), waits=[ld_tok])
            k = r.op("V", lambda v: v.bn_aggr(bnag_t[:, :], bnst_t[:, :]), waits=[k])
            k = r.op("V", lambda v: v.tensor_scalar(lnt_t[:, :], bnag_t[:, 1:2], EPS,
                                                    None, AT.add), waits=[k])
            k = r.op("V", lambda v: v.reciprocal(lnt2_t[:, :], lnt_t[:, :]), waits=[k])
            k = r.op("S", lambda s: s.activation(lnr_t[:, :], lnt2_t[:, :], AF.Sqrt),
                     waits=[k])
            for _ in range(2):
                k = r.op("V", lambda v: v.tensor_tensor(lnt2_t[:, :], lnr_t[:, :],
                                                        lnr_t[:, :], AT.mult), waits=[k])
                k = r.op("V", lambda v: v.tensor_tensor(lnt2_t[:, :], lnt2_t[:, :],
                                                        lnt_t[:, :], AT.mult), waits=[k])
                k = r.op("V", lambda v: v.tensor_scalar(lnt2_t[:, :], lnt2_t[:, :],
                                                        -0.5, 1.5, AT.mult, AT.add),
                         waits=[k])
                k = r.op("V", lambda v: v.tensor_tensor(lnr_t[:, :], lnr_t[:, :],
                                                        lnt2_t[:, :], AT.mult), waits=[k])
            return k

        def transpose_split(r, src_tile, dst_hi, dst_lo, col_base, t_src, bank_pair,
                            bank_guard):
            last = None
            for half in range(4):
                bkt = bank_pair[half % 2]
                tw = [t_src, bank_guard.get(bkt), c_id]
                mmtok = None
                for jj in range(4):
                    dc = half * 4 + jj
                    mmtok = r.op("T", lambda t, bkt=bkt, jj=jj, dc=dc: t.transpose(
                        pb[bkt][:, jj * 128:(jj + 1) * 128],
                        src_tile[:, dc * 128:(dc + 1) * 128], sident[:, :]),
                        waits=tw if jj == 0 else [], mark=(jj == 3))
                hi = r.op("S", lambda s, bkt=bkt, half=half: s.copy(
                    dap(dst_hi, (half * 4) * TPC + col_base,
                        [[NDC * TPC, 128], [TPC, 4], [1, 128]]),
                    pb[bkt][:, 0:512].rearrange("p (a b) -> p a b", a=4)),
                    waits=[mmtok])
                lo = r.op("V", lambda v, bkt=bkt, half=half: v.tensor_tensor(
                    dap(dst_lo, (half * 4) * TPC + col_base,
                        [[NDC * TPC, 128], [TPC, 4], [1, 128]]),
                    pb[bkt][:, 0:512].rearrange("p (a b) -> p a b", a=4),
                    dap(dst_hi, (half * 4) * TPC + col_base,
                        [[NDC * TPC, 128], [TPC, 4], [1, 128]]),
                    AT.subtract), waits=[mmtok, hi])
                bank_guard[bkt] = lo
                last = lo
            return last

        xn_free = [None, None]
        t_xnT = None
        for tt in range(8):
            sl = tt % 2
            ld = r.dma("Y", "qS", xtile[sl][:, :],
                       dap(x_in, tt * 128 * D, [[D, 128], [1, D]]),
                       waits=[xn_free[sl]])
            xs = xtile[sl]
            k = ln_chain(r, (None, lambda b, xs=xs: xs[:, b * 512:(b + 1) * 512]),
                         bnst, bnag, lnt, lnt2, lnr, ld)
            t_xn = r.op("V", lambda v, sl=sl: v.tensor_scalar(
                xn[sl][:, :], xtile[sl][:, :], bnag[:, 0:1], lnr[:, :],
                AT.subtract, AT.mult), waits=[k])
            t_xnT = transpose_split(r, xn[sl], xnT_h, xnT_l, tt * 128, t_xn,
                                    (6, 7), bank_owner)
            xn_free[sl] = t_xnT


        if dbg_xnt is not None:
            r.dma("G", "qG", dbg_xnt[0].ap(), xnT_h[:, :], waits=[t_xnT])
            r.dma("G", "qG", dbg_xnt[1].ap(), xnT_l[:, :], waits=[t_xnT])

        stLN.close()
        wqb_h = [SB(f"wqb_h{i}", [128, NDC * 128], BF16, stk=stA) for i in range(2)]
        wqb_l = [SB(f"wqb_l{i}", [128, NDC * 128], BF16, stk=stA) for i in range(2)]
        wraw = [SB(f"wraw{i}", [128, NDC * 128], stk=stA) for i in range(2)]
        wraw_free = [None, None]
        vraw = SB("vraw", [128, NDC * 256], stk=stA)
        vraw_free = None
        ev32 = [SB(f"ev32_{i}", [128, 512], stk=stA) for i in range(2)]
        evh = [SB(f"evh_{i}", [128, 512], BF16, stk=stA) for i in range(2)]
        evl = [SB(f"evl_{i}", [128, 512], BF16, stk=stA) for i in range(2)]
        vwh = SB("vwh", [128, NDC * 512], BF16, stk=stA)
        vwl = SB("vwl", [128, NDC * 512], BF16, stk=stA)

        ev_free = [None, None]
        kv_st = [None]
        q_st = [None]
        evi_ctr = [0]

        def evac(r, bank_tok, bank, dst_offs, store_list):
            ev_i = evi_ctr[0]
            evi_ctr[0] = (ev_i + 1) % 2
            t32 = r.op("S", lambda s, bank=bank, ev_i=ev_i:
                       s.copy(ev32[ev_i][:, :], bank[:, :]),
                       waits=[bank_tok, ev_free[ev_i]])
            th = r.op("V", lambda v, ev_i=ev_i: v.tensor_copy(
                evh[ev_i][:, :], ev32[ev_i][:, :]), waits=[t32])
            tl = r.op("V", lambda v, ev_i=ev_i: v.tensor_tensor(
                evl[ev_i][:, :], ev32[ev_i][:, :], evh[ev_i][:, :], AT.subtract),
                waits=[th])
            toks = []
            for (tgt, off_h, off_l, pat, c0, c1) in dst_offs:
                toks.append(r.dma("G", "qG", dap(tgt, off_h, pat),
                                  evh[ev_i][:, c0:c1], waits=[th]))
                toks.append(r.dma("G", "qG", dap(tgt, off_l, pat),
                                  evl[ev_i][:, c0:c1], waits=[tl]))
                store_list[0] = toks[-1]
            ev_free[ev_i] = toks
            return t32

        wslot_free = [None, None]

        quant_gen_holder = [None]

        def qdrive(n):
            g = quant_gen_holder[0]
            if g is None:
                return
            for _ in range(n):
                try:
                    next(g)
                except StopIteration:
                    quant_gen_holder[0] = None
                    return

        def proj_fm(r, m, dst, per, store_list):
            split_tok = {}

            def rec_split(ec):
                sl = ec % 2
                lr = r.dma("Y", "qS", wraw[sl][:, :],
                           dap(wqkvo, m * D * D + ec * 128,
                               [[D, 128], [128 * D, NDC], [1, 128]]),
                           waits=[wraw_free[sl], t_xnT])
                th_ = r.op("S", lambda s_, sl=sl: s_.copy(wqb_h[sl][:, :],
                                                          wraw[sl][:, :]),
                           waits=[lr, wslot_free[sl]])
                ll = r.op("V", lambda v, sl=sl: v.tensor_tensor(
                    wqb_l[sl][:, :], wraw[sl][:, :], wqb_h[sl][:, :], AT.subtract),
                    waits=[th_])
                wraw_free[sl] = ll
                split_tok[ec] = ll

            rec_split(0)
            rec_split(1)
            for ec in range(NDC):
                b0, b1 = 2 * (ec % 2), 2 * (ec % 2) + 1
                sl = ec % 2
                ll = split_tok.pop(ec)
                lh = ll
                mm_last = {}
                for dc in range(NDC):
                    wh = wqb_h[sl][:, dc * 128:(dc + 1) * 128]
                    wl = wqb_l[sl][:, dc * 128:(dc + 1) * 128]
                    mmw = [lh, ll, t_xnT]
                    if dc == 0:
                        mmw += [bank_owner.get(b0), bank_owner.get(b1)]
                    for tg, bk in ((0, b0), (1, b1)):
                        xh = xnT_h[:, dc * TPC + tg * 512:dc * TPC + tg * 512 + 512]
                        xl = xnT_l[:, dc * TPC + tg * 512:dc * TPC + tg * 512 + 512]
                        r.op("T", lambda t, wh=wh, bk=bk, xh=xh, dc=dc: t.matmul(
                            pb[bk][:, :], wh, xh,
                            start=(dc == 0), stop=False),
                            waits=mmw if tg == 0 else [], mark=False)
                        r.op("T", lambda t, wh=wh, bk=bk, xl=xl: t.matmul(
                            pb[bk][:, :], wh, xl,
                            start=False, stop=False), mark=False)
                        mm_last[tg] = r.op(
                            "T", lambda t, wl=wl, bk=bk, xh=xh, dc=dc: t.matmul(
                                pb[bk][:, :], wl, xh,
                                start=False, stop=(dc == NDC - 1)))
                wslot_free[sl] = mm_last[1]
                if ec + 2 < NDC:
                    rec_split(ec + 2)
                for tg, bk in ((0, b0), (1, b1)):
                    rank = ec // 2
                    sec0 = 0
                    off_h = (rank * per + sec0 * SEC
                             + ((ec % 2) * 128) * 1024 + tg * 512)
                    off_l = (rank * per + (sec0 + 1) * SEC
                             + ((ec % 2) * 128) * 1024 + tg * 512)
                    t32 = evac(r, mm_last[tg], pb[bk],
                               [(dst, off_h, off_l, [[1024, 128], [1, 512]], 0, 512)],
                               store_list)
                    bank_owner[bk] = t32
                qdrive(10)

        quant_gen_holder[0] = run_prep_jobs(quant_jobs, pf_in, gen=True,
                                            wbuf=qwbuf, obuf_h=qobuf,
                                            obuf_l=qobuf)
        proj_fm(r, 1, kv_in, KV_PER, kv_st)

        prev_v_mm = None
        prev_v_mm_mid = None
        for eg in range(4):
            vw_tok = {}
            for dch in range(2):
                lr = r.dma("Y", "qS",
                           dap(vraw, 0, [[NDC * 256, 128], [512, 8], [1, 512]]),
                           dap(wqkvo, 2 * D * D + (dch * 8 * 128) * D + eg * 512,
                               [[D, 128], [128 * D, 8], [1, 512]]),
                           waits=[(prev_v_mm_mid if dch == 0 else prev_v_mm),
                                  vraw_free, t_xnT])
                vh_ap = vwh[:, dch * 4096:(dch + 1) * 4096]
                vl_ap = vwl[:, dch * 4096:(dch + 1) * 4096]
                vr_ap = vraw[:, :]
                th_ = r.op("S", lambda s_, vh_ap=vh_ap, vr_ap=vr_ap:
                           s_.copy(vh_ap, vr_ap), waits=[lr])
                vw_tok[dch] = r.op("V", lambda v, vl_ap=vl_ap, vr_ap=vr_ap, vh_ap=vh_ap:
                                   v.tensor_tensor(vl_ap, vr_ap, vh_ap, AT.subtract),
                                   waits=[th_])
                vraw_free = vw_tok[dch]
            ld_last = vw_tok
            for tc in range(8):
                bk = 4 + tc % 2
                mml = None
                for dc in range(NDC):
                    xh = xnT_h[:, dc * TPC + tc * 128:dc * TPC + tc * 128 + 128]
                    xl = xnT_l[:, dc * TPC + tc * 128:dc * TPC + tc * 128 + 128]
                    vh = vwh[:, dc * 512:(dc + 1) * 512]
                    vl = vwl[:, dc * 512:(dc + 1) * 512]
                    mmw = ([ld_last[0], t_xnT, bank_owner.get(bk)] if dc == 0
                           else ([ld_last[1]] if dc == 8 else []))
                    r.op("T", lambda t, bk=bk, xh=xh, vh=vh, dc=dc: t.matmul(
                        pb[bk][:, :], xh, vh, start=(dc == 0), stop=False),
                        waits=mmw, mark=False)
                    r.op("T", lambda t, bk=bk, xh=xh, vl=vl: t.matmul(
                        pb[bk][:, :], xh, vl, start=False, stop=False), mark=False)
                    mml = r.op("T", lambda t, bk=bk, xl=xl, vh=vh, dc=dc: t.matmul(
                        pb[bk][:, :], xl, vh, start=False, stop=(dc == NDC - 1)))
                    if dc == 7 and tc == 7:
                        prev_v_mm_mid = mml
                prev_v_mm = mml
                dsts = []
                for half in range(2):
                    rank = 2 * eg + half
                    off_h = rank * KV_PER + 2 * SEC + (tc * 128) * 256
                    off_l = rank * KV_PER + 3 * SEC + (tc * 128) * 256
                    dsts.append((kv_in, off_h, off_l, [[256, 128], [1, 256]],
                                 half * 256, half * 256 + 256))
                bank_owner[bk] = evac(r, mml, pb[bk], dsts, kv_st)
                qdrive(10)

        t_a2a_kv = r.op("G", lambda g: g.collective_compute(
            "AllToAll", AT.bypass, replica_groups=[list(range(NCORES))],
            ins=[dap(kv_in, 0, [[1, NCORES * KV_PER]]).opt()
                 if False else kv_in.ap().opt()],
            outs=[kv_out.ap().opt()]),
            waits=r.qbar("qG"), ctr="cc")

        proj_fm(r, 0, q_in, Q_PER, q_st)
        t_a2a_q = r.op("G", lambda g: g.collective_compute(
            "AllToAll", AT.bypass, replica_groups=[list(range(NCORES))],
            ins=[q_in.ap().opt()], outs=[q_out.ap().opt()]),
            waits=r.qbar("qG"), ctr="cc")
        if quant_gen_holder[0] is not None:
            for _ in quant_gen_holder[0]:
                pass
            quant_gen_holder[0] = None
        t_agf = r.op("G", lambda g: g.collective_compute(
            "AllGather", AT.bypass, replica_groups=[list(range(NCORES))],
            ins=[pf_in.ap().opt()], outs=[pf_out.ap().opt()]),
            waits=r.qbar("qG"), ctr="cc")

        r.emit_block(nc, sems)
        stA.close()
        stB = ExitStack()

        smask = SB("smask", [128, 4 * 512], stk=stB)
        c_mask = r.dma("Y", "qS", dap(smask, 0, [[2048, 128], [512, 4], [1, 512]]),
                       dap(mask_in, 0, [[512, 128], [128 * 512, 4], [1, 512]]))
        qth = [SB(f"qth{i}", [128, 2 * 512], BF16, stk=stB) for i in range(2)]
        qtl = [SB(f"qtl{i}", [128, 2 * 512], BF16, stk=stB) for i in range(2)]
        kbh = [SB(f"kbh{i}", [128, 2 * 1024], BF16, stk=stB) for i in range(2)]
        kbl = [SB(f"kbl{i}", [128, 2 * 1024], BF16, stk=stB) for i in range(2)]
        vbh = [SB(f"vbh{i}", [128, 8 * 256], BF16, stk=stB) for i in range(2)]
        vbl = [SB(f"vbl{i}", [128, 8 * 256], BF16, stk=stB) for i in range(2)]
        p32 = [SB(f"p32_{i}", [128, 512], stk=stB) for i in range(3)]
        phb = [SB(f"phb{i}", [128, 512], BF16, stk=stB) for i in range(3)]
        plb = [SB(f"plb{i}", [128, 512], BF16, stk=stB) for i in range(3)]
        gate_sb = SB("gate_sb", [128, 512], stk=stB)
        rsum = SB("rsum", [1, 512], stk=stB)
        ao32 = SB("ao32", [128, 512], stk=stB)
        aoh = SB("aoh", [128, 512], BF16, stk=stB)
        aol = SB("aol", [128, 512], BF16, stk=stB)

        ao_last = None
        q_free = [None, None]
        kv_free = [None, None]
        s_free = [bank_owner.get(0), bank_owner.get(1), bank_owner.get(2)]
        p_free = [None, None, None]
        av_free = [bank_owner.get(3), bank_owner.get(4)]
        sums_free = [bank_owner.get(5)]
        gate_bank_free = bank_owner.get(6)
        bcast_bank_free = bank_owner.get(7)
        t_a1_prev = None
        t_bc_prev = None
        ao_buf_free = None

        uidx = 0
        pend_av = None
        t_a2a_aoA = None
        ao_lastA = None
        for cit, ci in enumerate(list(range(0, 16, 2)) + list(range(1, 16, 2))):
            nkb = 4 * (ci + 1)
            src = ci // 2
            cb = (ci % 2) * 512
            qp = cit % 2
            qh_ld = r.dma("Y", "qS", dap(qth[qp], 0, [[1024, 128], [512, 2], [1, 512]]),
                          dap(q_out, src * Q_PER + 0 * SEC + cb,
                              [[1024, 128], [128 * 1024, 2], [1, 512]]),
                          waits=[t_a2a_q, q_free[qp]])
            ql_ld = r.dma("Y", "qS", dap(qtl[qp], 0, [[1024, 128], [512, 2], [1, 512]]),
                          dap(q_out, src * Q_PER + 1 * SEC + cb,
                              [[1024, 128], [128 * 1024, 2], [1, 512]]),
                          waits=[t_a2a_q, q_free[qp]])
            for qd in range((nkb + 7) // 8):
                sl = qd % 2
                klds = []
                for part, buf in ((0, kbh[sl]), (1, kbl[sl])):
                    klds.append(r.dma(
                        "Y", "qS", dap(buf, 0, [[2048, 128], [1024, 2], [1, 1024]]),
                        dap(kv_out, qd * KV_PER + part * SEC,
                            [[1024, 128], [128 * 1024, 2], [1, 1024]]),
                        waits=[t_a2a_kv, kv_free[sl]]))
                for part, buf in ((0, vbh[sl]), (1, vbl[sl])):
                    klds.append(r.dma(
                        "Y", "qS", dap(buf, 0, [[2048, 128], [256, 8], [1, 256]]),
                        dap(kv_out, qd * KV_PER + (2 + part) * SEC,
                            [[256, 128], [128 * 256, 8], [1, 256]]),
                        waits=[t_a2a_kv, kv_free[sl]]))
                nkb_q = min(8, nkb - 8 * qd)
                for hl in range(2):
                    for kbloc in range(nkb_q):
                        kb = 8 * qd + kbloc
                        sb_ = uidx % 3
                        khap = dap(kbh[sl], hl * 1024 + kbloc * 128,
                                   [[2048, 128], [1, 128]])
                        klap = dap(kbl[sl], hl * 1024 + kbloc * 128,
                                   [[2048, 128], [1, 128]])
                        qhap = dap(qth[qp], hl * 512, [[1024, 128], [1, 512]])
                        qlap = dap(qtl[qp], hl * 512, [[1024, 128], [1, 512]])
                        sw = [klds, qh_ld, ql_ld, s_free[sb_]]
                        r.op("T", lambda t, sb_=sb_, khap=khap, qhap=qhap: t.matmul(
                            pb[sb_][:, :], khap, qhap, start=True, stop=False),
                            waits=sw, mark=False)
                        r.op("T", lambda t, sb_=sb_, khap=khap, qlap=qlap: t.matmul(
                            pb[sb_][:, :], khap, qlap, start=False, stop=False),
                            mark=False)
                        t_s = r.op("T", lambda t, sb_=sb_, klap=klap, qhap=qhap:
                                   t.matmul(pb[sb_][:, :], klap, qhap,
                                            start=False, stop=True))
                        t_e = r.op("S", lambda s, sb_=sb_: s.activation(
                            p32[sb_][:, :], pb[sb_][:, :], AF.Exp, scale=SCALE),
                            waits=[t_s, p_free[sb_]])
                        s_free[sb_] = t_e
                        t_pm = t_e
                        if kb >= nkb - 4:
                            o = kb - (nkb - 4)
                            t_pm = r.op("V", lambda v, sb_=sb_, o=o: v.tensor_tensor(
                                p32[sb_][:, :], p32[sb_][:, :],
                                smask[:, o * 512:(o + 1) * 512], AT.mult),
                                waits=[t_e, c_mask])
                        t_ph = r.op("V", lambda v, sb_=sb_: v.tensor_copy(
                            phb[sb_][:, :], p32[sb_][:, :]), waits=[t_pm])
                        t_pl = r.op("V", lambda v, sb_=sb_: v.tensor_tensor(
                            plb[sb_][:, :], p32[sb_][:, :], phb[sb_][:, :],
                            AT.subtract), waits=[t_ph])

                        def mk_av(hl=hl, sl=sl, kbloc=kbloc, kb=kb, nkb=nkb, sb_=sb_,
                                  t_ph=t_ph, t_pl=t_pl):
                            vhap = dap(vbh[sl], kbloc * 256 + hl * 128,
                                       [[2048, 128], [1, 128]])
                            vlap = dap(vbl[sl], kbloc * 256 + hl * 128,
                                       [[2048, 128], [1, 128]])
                            avw = [t_ph, t_pl, t_ones_b]
                            if kb == 0:
                                avw += [av_free[hl], sums_free[0]]
                            r.op("T", lambda t: t.matmul(
                                pb[3 + hl][:, :], vhap, phb[sb_][:, :],
                                start=(kb == 0), stop=False), waits=avw, mark=False)
                            r.op("T", lambda t: t.matmul(
                                pb[3 + hl][:, :], vhap, plb[sb_][:, :],
                                start=False, stop=False), mark=False)
                            r.op("T", lambda t: t.matmul(
                                pb[3 + hl][:, :], vlap, phb[sb_][:, :],
                                start=False, stop=(kb == nkb - 1)), mark=False)
                            r.op("T", lambda t: t.matmul(
                                pb[5][64 * hl + 0:64 * hl + 1, :], sones_b[:, :],
                                phb[sb_][:, :], start=(kb == 0), stop=(kb == nkb - 1),
                                tile_position=(0, 64 * hl + 0)), mark=False)
                            t_u2 = r.op("T", lambda t: t.matmul(
                                pb[5][64 * hl + 32:64 * hl + 33, :], sones_b[:, :],
                                plb[sb_][:, :], start=(kb == 0), stop=(kb == nkb - 1),
                                tile_position=(0, 64 * hl + 32)))
                            p_free[sb_] = t_u2
                            return t_u2

                        if pend_av is not None:
                            tok = pend_av()
                            kv_free[pend_sl] = tok
                        pend_av = mk_av
                        pend_sl = sl
                        uidx += 1
            t_g = {}
            for hl in range(2):
                qhap = dap(qth[qp], hl * 512, [[1024, 128], [1, 512]])
                qlap = dap(qtl[qp], hl * 512, [[1024, 128], [1, 512]])
                gw_ = [t_gwh, t_gwl, gate_bank_free]
                r.op("T", lambda t, qhap=qhap: t.matmul(
                    pb[6][:, (hl) * 0:512] if False else pb[6][:, :],
                    sgwt_h[:, :], qhap, start=True, stop=False),
                    waits=gw_, mark=False)
                r.op("T", lambda t, qlap=qlap: t.matmul(
                    pb[6][:, :], sgwt_h[:, :], qlap, start=False, stop=False),
                    mark=False)
                t_g[hl] = r.op("T", lambda t, qhap=qhap: t.matmul(
                    pb[6][:, :], sgwt_l[:, :], qhap, start=False, stop=True))
                if hl == 0:
                    tok = pend_av()
                    kv_free[pend_sl] = tok
                    pend_av = None
                    last_unit = tok
                t_sg = r.op("S", lambda s: s.activation(
                    gate_sb[:, :], pb[6][:, :], AF.Sigmoid, bias=sgb[:, :],
                    scale=1.0), waits=[t_g[hl], c_gb, t_a1_prev])
                gate_bank_free = t_sg
                t_ra0 = r.op("S", lambda s_, hl=hl: s_.copy(
                    rsum[:, :], pb[5][64 * hl:64 * hl + 1, :]),
                    waits=[last_unit, t_bc_prev])
                t_ra = r.op("V", lambda v, hl=hl: v.tensor_tensor(
                    rsum[:, :], rsum[:, :],
                    pb[5][64 * hl + 32:64 * hl + 33, :], AT.add),
                    waits=[t_ra0])
                t_rc = r.op("V", lambda v: v.reciprocal(
                    rsum[:, :], rsum[:, :]), waits=[t_ra])
                sums_free[0] = t_rc
                t_bc = r.op("T", lambda t: t.matmul(
                    pb[7][:, :], sones_f[:1, :], rsum[:1, :], start=True, stop=True),
                    waits=[t_rc, t_ones_f, bcast_bank_free])
                t_bc_prev = t_bc
                t_a1 = r.op("V", lambda v, hl=hl: v.tensor_tensor(
                    ao32[:, :], pb[3 + hl][:, :], gate_sb[:, :], AT.mult),
                    waits=[t_sg, last_unit, ao_buf_free])
                av_free[hl] = t_a1
                t_a1_prev = t_a1
                t_a2 = r.op("V", lambda v: v.tensor_tensor(
                    ao32[:, :], ao32[:, :], pb[7][:, :], AT.mult),
                    waits=[t_a1, t_bc])
                bcast_bank_free = t_a2
                t_h2 = r.op("S", lambda s: s.copy(aoh[:, :], ao32[:, :]),
                            waits=[t_a2])
                t_l2 = r.op("V", lambda v: v.tensor_tensor(
                    aol[:, :], ao32[:, :], aoh[:, :], AT.subtract), waits=[t_h2])
                dst = ci // 2
                ao_t = ao_inA if ci % 2 == 0 else ao_inB
                off_h = dst * AO2 + 0 * 256 * 512 + (hl * 128) * 512
                off_l = dst * AO2 + 1 * 256 * 512 + (hl * 128) * 512
                s_h = r.dma("G", "qG", dap(ao_t, off_h, [[512, 128], [1, 512]]),
                            aoh[:, :], waits=[t_h2])
                s_l = r.dma("G", "qG", dap(ao_t, off_l, [[512, 128], [1, 512]]),
                            aol[:, :], waits=[t_l2])
                ao_last = s_l
                ao_buf_free = [s_h, s_l]
            q_free[qp] = t_g[1]
            if cit == 7:
                t_a2a_aoA = r.op("G", lambda g: g.collective_compute(
                    "AllToAll", AT.bypass, replica_groups=[list(range(NCORES))],
                    ins=[ao_inA.ap().opt()], outs=[ao_outA.ap().opt()]),
                    waits=r.qbar("qG"), ctr="cc")

        t_a2a_aoB = r.op("G", lambda g: g.collective_compute(
            "AllToAll", AT.bypass, replica_groups=[list(range(NCORES))],
            ins=[ao_inB.ap().opt()], outs=[ao_outB.ap().opt()]),
            waits=r.qbar("qG"), ctr="cc")

        def ow_load(eg, first_wait, owraw, end_wait=None):
            toks = {}
            last = None
            for dch in range(2):
                lr = r.dma("Y", "qS",
                           dap(owraw, 0, [[NDC * 256, 128], [512, 8], [1, 512]]),
                           dap(wqkvo, 3 * D * D + (dch * 8 * 128) * D + eg * 512,
                               [[D, 128], [128 * D, 8], [1, 512]]),
                           waits=(first_wait if dch == 0 else [end_wait]) + [last])
                oh_ap = owh[:, dch * 4096:(dch + 1) * 4096]
                ol_ap = owl[:, dch * 4096:(dch + 1) * 4096]
                or_ap = owraw[:, :]
                th_ = r.op("S", lambda s_, oh_ap=oh_ap, or_ap=or_ap:
                           s_.copy(oh_ap, or_ap), waits=[lr])
                last = r.op("V", lambda v, ol_ap=ol_ap, or_ap=or_ap, oh_ap=oh_ap:
                            v.tensor_tensor(ol_ap, or_ap, oh_ap, AT.subtract),
                            waits=[th_])
                toks[dch] = last
            return toks
        owraw_B = SB("owraw_B", [128, NDC * 256], stk=stB)
        ow0_last = ow_load(0, [], owraw_B)

        r.emit_block(nc, sems)
        stB.close()
        stAB.close()
        stC = ExitStack()

        ach = [SB(f"ach{i}", [128, NDC * 128], BF16, stk=stC) for i in range(2)]
        acl = [SB(f"acl{i}", [128, NDC * 128], BF16, stk=stC) for i in range(2)]
        xsl = [SB(f"xsl{i}", [128, 512], stk=stC) for i in range(2)]
        owraw_C = SB("owraw_C", [128, NDC * 256], stk=stC)
        bnst2 = SB("bnst2", [128, 24], stk=stC)
        bnag2 = SB("bnag2", [128, 2], stk=stC)
        ln2t = SB("ln2t", [128, 1], stk=stC); ln2t2 = SB("ln2t2", [128, 1], stk=stC)
        ln2r = SB("ln2r", [128, 1], stk=stC)
        xn2 = SB("xn2", [128, D], stk=stC)

        x2buf = SB("bigA", [128, 8 * D], stk=stC)
        bank3 = {0: s_free[0], 1: s_free[1], 2: s_free[2], 3: av_free[0]}
        tp2 = {4: av_free[1], 5: sums_free[0]}
        ow_prev_mm = None
        ach_free = [None, None]
        xsl_free = [None, None]
        x2_toks = {}
        xn2T_last = None
        x2sp_st = []
        for tchalf in range(2):
          for eg in range(4):
            if tchalf == 0 and eg == 0:
                ow_lds = ow0_last
            else:
                ow_lds = ow_load(eg, [ow_mid_tok], owraw_C, end_wait=ow_prev_mm)
            for tc in range(tchalf * 4, tchalf * 4 + 4):
                bk = tc % 4
                asl = tc % 2
                ao_src = ao_outA if tc < 4 else ao_outB
                ao_tok = t_a2a_aoA if tc < 4 else t_a2a_aoB
                alds = []
                for part, buf in ((0, ach[asl]), (1, acl[asl])):
                    for rowhalf in range(2):
                        alds.append(r.dma(
                            "Y", "qS",
                            dap(buf, rowhalf * 128, [[NDC * 128, 128], [256, 8], [1, 128]]),
                            dap(ao_src, part * 256 * 512 + rowhalf * 128 * 512
                                + (tc % 4) * 128,
                                [[512, 128], [AO2, 8], [1, 128]]),
                            waits=[ao_tok, ach_free[asl]]))
                mml = None
                for dc in range(NDC):
                    mmw = []
                    if dc == 0:
                        mmw = [alds, ow_lds[0], bank3.get(bk)]
                    elif dc == 8:
                        mmw = [ow_lds[1]]
                    oh = owh[:, dc * 512:(dc + 1) * 512]
                    ol = owl[:, dc * 512:(dc + 1) * 512]
                    ah_ = ach[asl][:, dc * 128:(dc + 1) * 128]
                    al_ = acl[asl][:, dc * 128:(dc + 1) * 128]
                    r.op("T", lambda t, ah_=ah_, bk=bk, oh=oh, dc=dc: t.matmul(
                        pb[bk][:, :], ah_, oh, start=(dc == 0),
                        stop=False), waits=mmw, mark=False)
                    r.op("T", lambda t, ah_=ah_, bk=bk, ol=ol: t.matmul(
                        pb[bk][:, :], ah_, ol, start=False, stop=False),
                        mark=False)
                    mml = r.op("T", lambda t, al_=al_, bk=bk, oh=oh, dc=dc: t.matmul(
                        pb[bk][:, :], al_, oh, start=False,
                        stop=(dc == NDC - 1)))
                    if dc == 7:
                        ow_mid_tok = mml
                ach_free[asl] = mml
                xs = tc % 2
                xld = r.dma("Y", "qS", xsl[xs][:, :],
                            dap(x_in, tc * 128 * D + eg * 512, [[D, 128], [1, 512]]),
                            waits=[xsl_free[xs]])
                t_x2 = r.op("V", lambda v, bk=bk, xs=xs, tc=tc, eg=eg: v.tensor_tensor(
                    x2buf[:, tc * D + eg * 512:tc * D + eg * 512 + 512],
                    pb[bk][:, :], xsl[xs][:, :], AT.add), waits=[mml, xld])
                bank3[bk] = t_x2
                xsl_free[xs] = t_x2
                x2_toks[(tc, eg)] = t_x2
            ow_prev_mm = mml

        for tc in range(8):
            w0 = [x2_toks[(tc, eg)] for eg in range(4)]
            sp = r.dma("G", "qG", dap(x2_sp, tc * 128 * D, [[D, 128], [1, D]]),
                       x2buf[:, tc * D:(tc + 1) * D], waits=w0)
            x2sp_st.append(sp)
            xb = x2buf
            k = ln_chain(r, (None, lambda b, tc=tc, xb=xb:
                             xb[:, tc * D + b * 512:tc * D + (b + 1) * 512]),
                         bnst2, bnag2, ln2t, ln2t2, ln2r, w0[-1])
            for t0 in w0:
                pass
            t_xn2 = r.op("V", lambda v, tc=tc: v.tensor_scalar(
                xn2[:, :], x2buf[:, tc * D:(tc + 1) * D], bnag2[:, 0:1],
                ln2r[:, :], AT.subtract, AT.mult), waits=[k] + w0)
            xn2T_last = transpose_split(r, xn2, xnT_h, xnT_l, tc * 128, t_xn2,
                                        (4, 5), tp2)

        r.emit_block(nc, sems)
        stC.close()
        stBC.close()
        stD = ExitStack()

        for _eng, _nopfn in (("T", lambda t: t.nop()), ("V", lambda v: v.engine_nop()),
                             ("S", lambda s_: s_.nop()), ("G", lambda g: g.engine_nop()),
                             ("Y", lambda y: y.nop())):
            r.op(_eng, _nopfn, waits=list(x2sp_st), mark=False)
        sds_row = SB("sds_row", [1, D], stk=stD)
        c_dsr = r.dma("Y", "qS", sds_row[:, :], ds_in.ap())
        sds_b = SB("sds_b", [128, 512], stk=stD)
        t_dsb_cur = None
        dsb_hg = [-1]

        def ensure_dsb(hg):
            nonlocal t_dsb_cur
            if dsb_hg[0] == hg:
                return t_dsb_cur
            mmw = [c_dsr, t_ones_f, bank_f.get(4)]
            if t_dsb_cur is not None:
                mmw.append(t_dsb_cur)
            mm = r.op("T", lambda t, hg=hg: t.matmul(
                pb[4][:, :], sones_f[:1, :],
                sds_row[:1, hg * 512:(hg + 1) * 512], start=True, stop=True),
                waits=mmw)
            t_dsb_cur = r.op("V", lambda v: v.tensor_copy(sds_b[:, :], pb[4][:, :]),
                             waits=[mm])
            bank_f[4] = t_dsb_cur
            dsb_hg[0] = hg
            return t_dsb_cur

        fuse_h = SB("fuse_h", [128, NIC * 512], BF16, stk=stD)
        fuse_l = SB("fuse_l", [128, NIC * 512], BF16, stk=stD)
        gwb = [SB(f"gwb{i}", [128, NDC * 256], F8, stk=stD) for i in range(2)]
        uwb = [SB(f"uwb{i}", [128, NDC * 256], F8, stk=stD) for i in range(1)]
        dwb = [SB(f"dwb{i}", [128, 512], F8, stk=stD) for i in range(2)]
        fsil = SB("fsil", [128, 512], stk=stD)
        fup = SB("fup", [128, 512], stk=stD)
        x2t = [SB(f"x2t{i}", [128, 512], stk=stD) for i in range(2)]
        otile = [fsil, fup]

        out_last = None
        gw_free = [None, None]
        uw_free = None
        dw_free = [None, None]
        fs_free = None
        fu_free = None
        bank_f = {0: bank3[0], 1: bank3[1], 2: bank3[2], 3: bank3[3],
                  4: tp2[4], 5: tp2[5],
                  6: gate_bank_free, 7: bcast_bank_free}
        x2t_free = [None, None]
        ot_free = [None, None]
        prev_down = None
        for grp in range(2):
            fuse_toks = {}
            for ic in range(NIC):
                bkG = 4 + (ic % 2) * 2
                bkU = bkG + 1
                icg = ic // 2
                gsl = icg % 2
                if ic % 2 == 0:
                    rank = ic // 6
                    coli = ((icg * 2) % 6) * 128
                    g1 = r.dma("Y", "qS",
                               dap(gwb[gsl], 0, [[NDC * 256, 128], [256, NDC], [1, 256]]),
                               dap(pf_out, rank * PF_N + 0 * D * 768
                                   + coli, [[768, 128], [128 * 768, NDC], [1, 256]]),
                               waits=[gw_free[gsl], t_agf])
                    g2 = r.dma("Y", "qS",
                               dap(uwb[0], 0, [[NDC * 256, 128], [256, NDC], [1, 256]]),
                               dap(pf_out, rank * PF_N + 1 * D * 768
                                   + coli, [[768, 128], [128 * 768, NDC], [1, 256]]),
                               waits=[uw_free])
                    gw_lds = (g1, g2)
                mmlG = mmlU = None
                for dc in range(NDC):
                    mmw = [gw_lds[0], xn2T_last]
                    if dc == 0:
                        mmw += [bank_f.get(bkG)]
                    gq = gwb[gsl][:, dc * 256 + (ic % 2) * 128:dc * 256 + (ic % 2) * 128 + 128]
                    xh = xnT_h[:, dc * TPC + grp * 512:dc * TPC + grp * 512 + 512]
                    xl = xnT_l[:, dc * TPC + grp * 512:dc * TPC + grp * 512 + 512]
                    r.op("T", lambda t, gq=gq, bkG=bkG, xh=xh, dc=dc: t.matmul(
                        pb[bkG][:, :], gq, xh, start=(dc == 0),
                        stop=False), waits=mmw, mark=False)
                    mmlG = r.op("T", lambda t, gq=gq, bkG=bkG, xl=xl, dc=dc: t.matmul(
                        pb[bkG][:, :], gq, xl, start=False,
                        stop=(dc == NDC - 1)))
                for dc in range(NDC):
                    mmw = [gw_lds[1], xn2T_last]
                    if dc == 0:
                        mmw += [bank_f.get(bkU)]
                    uq = uwb[0][:, dc * 256 + (ic % 2) * 128:dc * 256 + (ic % 2) * 128 + 128]
                    xh = xnT_h[:, dc * TPC + grp * 512:dc * TPC + grp * 512 + 512]
                    xl = xnT_l[:, dc * TPC + grp * 512:dc * TPC + grp * 512 + 512]
                    r.op("T", lambda t, uq=uq, bkU=bkU, xh=xh, dc=dc: t.matmul(
                        pb[bkU][:, :], uq, xh, start=(dc == 0),
                        stop=False), waits=mmw, mark=False)
                    mmlU = r.op("T", lambda t, uq=uq, bkU=bkU, xl=xl, dc=dc: t.matmul(
                        pb[bkU][:, :], uq, xl, start=False,
                        stop=(dc == NDC - 1)))
                if ic % 2 == 1:
                    gw_free[gsl] = mmlG
                    uw_free = mmlU
                t_si = r.op("S", lambda s, bkG=bkG, ic=ic: s.activation(
                    fsil[:, :], pb[bkG][:, :], AF.Silu, scale=sgs[:, ic:ic + 1]),
                    waits=[mmlG, c_gs, fs_free]
                    + (list(ot_free) if ic == 0 else []))
                bank_f[bkG] = t_si
                t_up = r.op("V", lambda v, bkU=bkU, ic=ic: v.tensor_scalar(
                    fup[:, :], pb[bkU][:, :], sus[:, ic:ic + 1], None, AT.mult),
                    waits=[mmlU, c_us, fu_free]
                    + (list(ot_free) if ic == 0 else []))
                bank_f[bkU] = t_up
                t_f = r.op("V", lambda v: v.tensor_tensor(
                    fup[:, :], fup[:, :], fsil[:, :], AT.mult), waits=[t_si, t_up])
                fs_free = t_f
                t_fh = r.op("S", lambda s, ic=ic: s.copy(
                    fuse_h[:, ic * 512:(ic + 1) * 512], fup[:, :]),
                    waits=[t_f, prev_down if ic == 0 else None])
                t_fl = r.op("V", lambda v, ic=ic: v.tensor_tensor(
                    fuse_l[:, ic * 512:(ic + 1) * 512], fup[:, :],
                    fuse_h[:, ic * 512:(ic + 1) * 512], AT.subtract),
                    waits=[t_fh, prev_down if ic == 0 else None])
                fu_free = t_fl
                fuse_toks[ic] = t_fl
            for hg in range(4):
                mml = {tc: None for tc in range(4)}
                for ic in range(NIC):
                    sl = ic % 2
                    off = (2 * hg) * PF_N + SEC_DN + (ic * 128) * 256
                    dld = r.dma("Y", "qS",
                                dap(dwb[sl], 0, [[512, 128], [256, 2], [1, 256]]),
                                dap(pf_out, off, [[256, 128], [PF_N, 2], [1, 256]]),
                                waits=[dw_free[sl], t_agf])
                    for tc in range(4):
                        bk = tc
                        mmw = [dld, fuse_toks[ic]]
                        if ic == 0:
                            mmw.append(bank_f.get(bk))
                        fh = fuse_h[:, ic * 512 + tc * 128:ic * 512 + tc * 128 + 128]
                        fl = fuse_l[:, ic * 512 + tc * 128:ic * 512 + tc * 128 + 128]
                        r.op("T", lambda t, bk=bk, fh=fh, sl=sl, ic=ic: t.matmul(
                            pb[bk][:, :], fh, dwb[sl][:, :], start=(ic == 0),
                            stop=False), waits=mmw, mark=False)
                        mml[tc] = r.op("T", lambda t, bk=bk, fl=fl, sl=sl, ic=ic:
                                       t.matmul(pb[bk][:, :], fl, dwb[sl][:, :],
                                                start=False, stop=(ic == NIC - 1)))
                    dw_free[sl] = mml[3]
                prev_down = mml[3]
                t_dsb_now = ensure_dsb(hg)
                for tc in range(4):
                    xs = tc % 2
                    xld = r.dma("Y", "qS", x2t[xs][:, :],
                                dap(x2_sp, (grp * 512 + tc * 128) * D + hg * 512,
                                    [[D, 128], [1, 512]]),
                                waits=[x2t_free[xs], x2sp_st[grp * 4 + tc]])
                    ots = tc % 2
                    t_d1 = r.op("V", lambda v, tc=tc, ots=ots: v.tensor_tensor(
                        otile[ots][:, :], pb[tc][:, :],
                        sds_b[:, :], AT.mult),
                        waits=[mml[tc], t_dsb_now, ot_free[ots]])
                    bank_f[tc] = t_d1
                    t_d2 = r.op("V", lambda v, xs=xs, ots=ots: v.tensor_tensor(
                        otile[ots][:, :], otile[ots][:, :], x2t[xs][:, :], AT.add),
                        waits=[t_d1, xld])
                    x2t_free[xs] = t_d2
                    so = r.dma("G", "qG",
                               dap(out_t, (grp * 512 + tc * 128) * D + hg * 512,
                                   [[D, 128], [1, 512]]),
                               otile[ots][:, :], waits=[t_d2])
                    ot_free[ots] = so
                    out_last = so

        for tap_t, src_t in dbg_taps:
            out_last = r.dma("G", "qG", tap_t.ap(), src_t.ap(), waits=[out_last])

        r.op("G", lambda g: g.engine_nop(), waits=[out_last], mark=False)

        r.emit_block(nc, sems)
        stD.close()

    return nc


def prepare_in_maps(inputs):
    x = np.ascontiguousarray(np.asarray(inputs["L0_attn_x"], dtype=np.float32)).reshape(S, D)
    qw = np.asarray(inputs["L0_attn_q_w"], np.float32)
    kw = np.asarray(inputs["L0_attn_k_w"], np.float32)
    vw = np.asarray(inputs["L0_attn_v_w"], np.float32)
    ow = np.asarray(inputs["L0_attn_o_w"], np.float32)
    gwt = np.ascontiguousarray(np.asarray(inputs["L0_attn_gate_w"], np.float32).T)
    gb = np.ascontiguousarray(np.asarray(inputs["L0_attn_gate_b"], np.float32).reshape(HD, 1))
    gw_f = np.asarray(inputs["L0_ffn_gate_w"], np.float32)
    uw_f = np.asarray(inputs["L0_ffn_up_w"], np.float32)
    dw_f = np.asarray(inputs["L0_ffn_down_w"], np.float32)
    gs = np.ascontiguousarray(np.asarray(inputs["L0_ffn_gate_scale"], np.float32).reshape(NIC, 128).T)
    us = np.ascontiguousarray(np.asarray(inputs["L0_ffn_up_scale"], np.float32).reshape(NIC, 128).T)
    ds = np.ascontiguousarray(np.asarray(inputs["L0_ffn_down_scale"], np.float32).reshape(1, D))
    ident = np.eye(128, dtype=np.float32)
    k_idx = np.arange(128).reshape(1, 128, 1)
    q_idx = np.arange(512).reshape(1, 1, 512)
    o_idx = np.arange(4).reshape(4, 1, 1)
    masks = ((128 * o_idx + k_idx) <= q_idx).astype(np.float32)

    qkvoT = np.stack([np.ascontiguousarray(w.T) for w in (qw, kw, vw, ow)])
    in_maps = []
    for c in range(NCORES):
        in_maps.append({
            "x": np.ascontiguousarray(x[c * TPC:(c + 1) * TPC]),
            "wqkvo": qkvoT,
            "gu": np.ascontiguousarray(
                np.stack([gw_f[:, c * 768:(c + 1) * 768], uw_f[:, c * 768:(c + 1) * 768]])),
            "dn": np.ascontiguousarray(dw_f[:, c * 256:(c + 1) * 256]),
            "gwt": gwt, "gb": gb, "gs": gs, "us": us, "ds": ds,
            "ident": ident, "masks": masks,
        })
    return in_maps


def kernel(**inputs):
    in_maps = prepare_in_maps(inputs)
    nc = build_program(DBG)
    res = run_bass_kernel_spmd(nc, in_maps, core_ids=list(range(NCORES)))
    out = np.concatenate([np.asarray(r["out"]) for r in res.results], axis=0)
    return out.reshape(1, S, D).astype(np.float32)


# version 28
# speedup vs baseline: 1.0072x; 1.0072x over previous
from contextlib import ExitStack

import numpy as np

import concourse.bass as bass
import concourse.mybir as mybir
from concourse.bass_utils import run_bass_kernel_spmd
from concourse.aot_env import allow_ici_all_to_all

F32 = mybir.dt.float32
BF16 = mybir.dt.bfloat16
F8 = mybir.dt.float8e4
AT = mybir.AluOpType
AF = mybir.ActivationFunctionType

S = 8192
D = 2048
NH = 16
HD = 128
FF = 6144
NCORES = 8
TPC = S // NCORES
NDC = D // 128
NIC = FF // 128
EPS = 1e-5
SCALE = float(np.float32(1.0) / np.sqrt(np.float32(128.0)))

SEC_DN = 2 * D * 768
PF_N = SEC_DN + FF * 256

SEC = 256 * 1024
KV_PER = 4 * SEC
AO_PER = 2 * SEC

DBG = ()


class Rec:

    ENGS = ("T", "V", "S", "G", "Y")

    NSEM = 8

    def __init__(self):
        self.ops = {e: [] for e in self.ENGS}
        self.cnt = {"T": 0, "V": 0, "S": 0, "G": 0, "Y": 0, "cc": 0}
        self.step = {"T": 1, "V": 1, "S": 1, "G": 1, "Y": 1, "cc": 1}
        for q in ("qS", "qG", "qA"):
            self.cnt[q] = 0
            for i in range(self.NSEM):
                self.cnt[f"{q}{i}"] = 0
                self.step[f"{q}{i}"] = 16

    def qbar(self, queue):
        return [(f"{queue}{i}", self.cnt[f"{queue}{i}"] * 16)
                for i in range(self.NSEM) if self.cnt[f"{queue}{i}"] > 0]

    def op(self, eng, fn, waits=(), mark=True, ctr=None):
        ctr = ctr or eng
        flat = []
        for w in waits:
            if w is None:
                continue
            if isinstance(w, list):
                flat.extend(x for x in w if x is not None)
            else:
                flat.append(w)
        waits = flat
        if eng in ("V", "S", "G") and self.cnt[eng] > 0:
            waits.append((eng, self.cnt[eng] * self.step[eng]))
        for (cn, cv) in waits:
            assert cv <= self.cnt[cn] * self.step[cn], (
                f"wait on future token {cn}={cv} (now {self.cnt[cn] * self.step[cn]})")
        tok = None
        if mark:
            self.cnt[ctr] += 1
            tok = (ctr, self.cnt[ctr] * self.step[ctr])
        self.ops[eng].append((waits, fn, ctr if mark else None))
        return tok

    def dma(self, eng, queue, out, in_, waits=()):
        i = self.cnt[queue]
        self.cnt[queue] += 1
        sem = f"{queue}{i % self.NSEM}"
        w = [x for x in waits if x is not None]
        if i >= self.NSEM:
            w.append((sem, 16 * (i // self.NSEM)))

        def fn(e, out=out, in_=in_):
            return e.dma_start(out=out, in_=in_)
        return self.op(eng, fn, waits=w, ctr=sem)

    def emit_block(self, nc, sems):
        with nc.Block() as blk:
            def mk(ename):
                ops = self.ops[ename]

                def body(eh):
                    wm = {}
                    for waits, fn, ctr in ops:
                        for (cn, cv) in waits:
                            if wm.get(cn, 0) < cv:
                                eh.wait_ge(sems[cn], cv)
                                wm[cn] = cv
                        instr = fn(eh)
                        if ctr is not None:
                            instr.then_inc(sems[ctr], self.step[ctr])
                return body

            for ename, reg in (("T", blk.tensor), ("V", blk.vector),
                               ("S", blk.scalar), ("G", blk.gpsimd),
                               ("Y", blk.sync)):
                if self.ops[ename]:
                    reg(mk(ename))
        self.ops = {e: [] for e in self.ENGS}


def dap(t, off, pattern):
    return bass.AP(t, off, pattern)


def build_program(dbg=()):
    with allow_ici_all_to_all():
        return _build_program(dbg)


def _build_program(dbg=()):
    nc = bass.Bass("TRN2", target_bir_lowering=False)

    x_in = nc.dram_tensor("x", [TPC, D], F32, kind="ExternalInput")
    wqkvo = nc.dram_tensor("wqkvo", [4, D, D], F32, kind="ExternalInput")
    gu_in = nc.dram_tensor("gu", [2, D, 768], F32, kind="ExternalInput")
    dn_in = nc.dram_tensor("dn", [FF, 256], F32, kind="ExternalInput")
    gwt_in = nc.dram_tensor("gwt", [HD, HD], F32, kind="ExternalInput")
    gb_in = nc.dram_tensor("gb", [HD, 1], F32, kind="ExternalInput")
    gs_in = nc.dram_tensor("gs", [128, NIC], F32, kind="ExternalInput")
    us_in = nc.dram_tensor("us", [128, NIC], F32, kind="ExternalInput")
    ds_in = nc.dram_tensor("ds", [1, D], F32, kind="ExternalInput")
    id_in = nc.dram_tensor("ident", [128, 128], F32, kind="ExternalInput")
    mask_in = nc.dram_tensor("masks", [4, 128, 512], F32, kind="ExternalInput")
    out_t = nc.dram_tensor("out", [TPC, D], F32, kind="ExternalOutput")

    pf_in = nc.dram_tensor("pf_in", [PF_N], F8)
    pf_out = nc.dram_tensor("pf_out", [NCORES * PF_N], F8)
    kv_in = nc.dram_tensor("kv_in", [NCORES * KV_PER], BF16)
    kv_out = nc.dram_tensor("kv_out", [NCORES * KV_PER], BF16)
    Q_PER = 2 * SEC
    q_in = nc.dram_tensor("q_in", [NCORES * Q_PER], BF16)
    q_out = nc.dram_tensor("q_out", [NCORES * Q_PER], BF16)
    AO2 = 2 * 256 * 512
    ao_inA = nc.dram_tensor("ao_inA", [NCORES * AO2], BF16)
    ao_outA = nc.dram_tensor("ao_outA", [NCORES * AO2], BF16)
    ao_inB = nc.dram_tensor("ao_inB", [NCORES * AO2], BF16)
    ao_outB = nc.dram_tensor("ao_outB", [NCORES * AO2], BF16)
    x2_sp = nc.dram_tensor("x2_sp", [TPC, D], F32)

    dbg_taps = []
    if "kvq" in dbg:
        dbg_taps.append((nc.dram_tensor("dbg_kv", [NCORES * KV_PER], BF16,
                                        kind="ExternalOutput"), kv_out))
        pass
    if "prep" in dbg:
        pass
        dbg_taps.append((nc.dram_tensor("dbg_prepf", [NCORES * PF_N], F8,
                                        kind="ExternalOutput"), pf_out))

    if "pre" in dbg:
        dbg_taps.append((nc.dram_tensor("dbg_kvin", [NCORES * KV_PER], BF16,
                                        kind="ExternalOutput"), kv_in))
        dbg_taps.append((nc.dram_tensor("dbg_qin", [NCORES * Q_PER], BF16,
                                        kind="ExternalOutput"), q_in))
    dbg_xnt = None
    if "xnt" in dbg:
        dbg_xnt = (nc.dram_tensor("dbg_xnth", [128, NDC * TPC], BF16,
                                  kind="ExternalOutput"),
                   nc.dram_tensor("dbg_xntl", [128, NDC * TPC], BF16,
                                  kind="ExternalOutput"))
    if "x2" in dbg:
        dbg_taps.append((nc.dram_tensor("dbg_x2", [TPC, D], F32,
                                        kind="ExternalOutput"), x2_sp))

    with ExitStack() as st:
        sems = {}
        sem_names = ["T", "V", "S", "G", "Y", "cc"]
        for q in ("qS", "qG", "qA"):
            sem_names += [f"{q}{i}" for i in range(Rec.NSEM)]
        for name in sem_names:
            sems[name] = st.enter_context(nc.semaphore("sem_" + name))

        pb = [st.enter_context(nc.psum_tensor(f"pb{i}", [128, 512], F32))
              for i in range(8)]

        def SB(name, shape, dt=F32, stk=None):
            return (stk or st).enter_context(nc.sbuf_tensor(name, shape, dt))

        sident = SB("sident", [128, 128])
        sgwt32 = SB("sgwt32", [128, 128])
        sgwt_h = SB("sgwt_h", [128, 128], BF16)
        sgwt_l = SB("sgwt_l", [128, 128], BF16)
        sgb = SB("sgb", [128, 1])
        sgs = SB("sgs", [128, NIC])
        sus = SB("sus", [128, NIC])
        sones_b = SB("sones_b", [128, 1], BF16)
        sones_f = SB("sones_f", [1, 128])

        xnT_h = SB("xnT_h", [128, NDC * TPC], BF16)
        xnT_l = SB("xnT_l", [128, NDC * TPC], BF16)
        r = Rec()
        stBC = ExitStack()
        owh = SB("owh", [128, NDC * 512], BF16, stk=stBC)
        owl = SB("owl", [128, NDC * 512], BF16, stk=stBC)

        stA = ExitStack()
        stAB = ExitStack()

        c_id = r.dma("Y", "qS", sident[:, :], id_in.ap())
        c_gb = r.dma("Y", "qS", sgb[:, :], gb_in.ap())
        c_gs = r.dma("Y", "qS", sgs[:, :], gs_in.ap())
        c_us = r.dma("Y", "qS", sus[:, :], us_in.ap())
        c_gwt = r.dma("Y", "qS", sgwt32[:, :], gwt_in.ap())
        t_ones_b = r.op("G", lambda g: g.memset(sones_b[:, :], 1.0))
        t_ones_f = r.op("G", lambda g: g.memset(sones_f[:, :], 1.0))

        t_gwh = r.op("V", lambda v: v.tensor_copy(sgwt_h[:, :], sgwt32[:, :]),
                     waits=[c_gwt])
        t_gwl = r.op("V", lambda v: v.tensor_tensor(sgwt_l[:, :], sgwt32[:, :],
                                                    sgwt_h[:, :], AT.subtract),
                     waits=[t_gwh])
        bank_owner = {j: None for j in range(8)}

        t_c = SB("t_c", [128, 768], stk=stAB); t_d = SB("t_d", [128, 768], stk=stAB)
        t_h = SB("t_h", [128, 768], stk=stAB)
        t_m = SB("t_m", [128, 768], mybir.dt.uint8, stk=stAB)
        qwbuf = [SB(f"qwbuf{i}", [128, 768], stk=stAB) for i in range(2)]
        qobuf = [SB(f"qobuf{i}", [128, 768], F8, stk=stAB) for i in range(2)]
        stLN = ExitStack()


        quant_jobs = []
        for m in range(2):
            for dc in range(NDC):
                quant_jobs.append(("quant",
                                   dap(gu_in, (m * D + dc * 128) * 768, [[768, 128], [1, 768]]),
                                   768,
                                   m * D * 768 + dc * 128 * 768, None, 768))
        for icc in range(NIC):
            quant_jobs.append(("quant",
                               dap(dn_in, icc * 128 * 256, [[256, 128], [1, 256]]),
                               256, SEC_DN + icc * 128 * 256, None, 256))

        slot_free = [None, None]
        prep_last = None

        prep_marks = []

        def run_prep_jobs(jobs, tgt, ldeng="Y", ldq="qS", gen=False,
                          wbuf=None, obuf_h=None, obuf_l=None):
          nonlocal prep_last
          for j, (kind, src, wid, off_h, off_l, rstr) in enumerate(jobs):
            prep_in = tgt
            sl = j % 2
            ld = r.dma(ldeng, ldq, wbuf[sl][:, 0:wid], src, waits=[slot_free[sl]])
            if gen:
                yield
            if kind == "split":
                th = r.op("S", lambda s_, sl=sl, wid=wid: s_.copy(
                    obuf_h[sl][:, 0:wid], wbuf[sl][:, 0:wid]), waits=[ld])
                tl = r.op("V", lambda v, sl=sl, wid=wid: v.tensor_tensor(
                    obuf_l[sl][:, 0:wid], wbuf[sl][:, 0:wid],
                    obuf_h[sl][:, 0:wid], AT.subtract), waits=[th])
                r.dma("G", "qG", dap(prep_in, off_h, [[rstr, 128], [1, wid]]),
                      obuf_h[sl][:, 0:wid], waits=[th])
                s2 = r.dma("G", "qG", dap(prep_in, off_l, [[rstr, 128], [1, wid]]),
                           obuf_l[sl][:, 0:wid], waits=[tl])
                slot_free[sl] = s2
                prep_last = s2
                prep_marks.append(s2)
            else:
                k1 = r.op("V", lambda v, sl=sl, wid=wid: v.tensor_scalar(
                    t_c[:, 0:wid], wbuf[sl][:, 0:wid], 1048577.0, None, AT.mult),
                    waits=[ld])
                if gen:
                    yield
                k2 = r.op("V", lambda v, sl=sl, wid=wid: v.tensor_tensor(
                    t_d[:, 0:wid], t_c[:, 0:wid], wbuf[sl][:, 0:wid], AT.subtract),
                    waits=[k1])
                if gen:
                    yield
                k3 = r.op("V", lambda v, wid=wid: v.tensor_tensor(
                    t_h[:, 0:wid], t_c[:, 0:wid], t_d[:, 0:wid], AT.subtract),
                    waits=[k2])
                if gen:
                    yield
                k4 = r.op("V", lambda v, sl=sl, wid=wid: v.tensor_scalar(
                    t_c[:, 0:wid], wbuf[sl][:, 0:wid], 24576.0, -24576.0,
                    AT.add, AT.add), waits=[ld])
                if gen:
                    yield
                k5a = r.op("V", lambda v, sl=sl, wid=wid: v.tensor_tensor(
                    t_d[:, 0:wid], wbuf[sl][:, 0:wid], wbuf[sl][:, 0:wid],
                    AT.mult), waits=[ld, k3])
                if gen:
                    yield
                k5 = r.op("V", lambda v, wid=wid: v.tensor_scalar(
                    t_m[:, 0:wid], t_d[:, 0:wid], 2.0 ** -12, None,
                    AT.is_ge), waits=[k5a])
                if gen:
                    yield
                k6 = r.op("V", lambda v, sl=sl, wid=wid: v.select(
                    obuf_h[sl][:, 0:wid], t_m[:, 0:wid], t_h[:, 0:wid],
                    t_c[:, 0:wid], add_drain=True), waits=[k3, k4, k5])
                if gen:
                    yield
                s1 = r.dma("G", "qG", dap(prep_in, off_h, [[rstr, 128], [1, wid]]),
                           obuf_h[sl][:, 0:wid], waits=[k6])
                slot_free[sl] = s1
                prep_last = s1



        xtile = [SB(f"xtile{i}", [128, D], stk=stLN) for i in range(2)]
        bnst = SB("bnst", [128, 24], stk=stLN)
        bnag = SB("bnag", [128, 2], stk=stLN)
        lnt = SB("lnt", [128, 1], stk=stLN); lnt2 = SB("lnt2", [128, 1], stk=stLN)
        lnr = SB("lnr", [128, 1], stk=stLN)
        xn = [SB(f"xn{i}", [128, D], stk=stLN) for i in range(2)]

        def ln_chain(r, x_ap, bnst_t, bnag_t, lnt_t, lnt2_t, lnr_t, ld_tok):
            k = None
            for b in range(4):
                k = r.op("V", lambda v, b=b: v.bn_stats(
                    bnst_t[:, b * 6:(b + 1) * 6], x_ap[1](b)), waits=[ld_tok])
            k = r.op("V", lambda v: v.bn_aggr(bnag_t[:, :], bnst_t[:, :]), waits=[k])
            k = r.op("V", lambda v: v.tensor_scalar(lnt_t[:, :], bnag_t[:, 1:2], EPS,
                                                    None, AT.add), waits=[k])
            k = r.op("V", lambda v: v.reciprocal(lnt2_t[:, :], lnt_t[:, :]), waits=[k])
            k = r.op("S", lambda s: s.activation(lnr_t[:, :], lnt2_t[:, :], AF.Sqrt),
                     waits=[k])
            for _ in range(2):
                k = r.op("V", lambda v: v.tensor_tensor(lnt2_t[:, :], lnr_t[:, :],
                                                        lnr_t[:, :], AT.mult), waits=[k])
                k = r.op("V", lambda v: v.tensor_tensor(lnt2_t[:, :], lnt2_t[:, :],
                                                        lnt_t[:, :], AT.mult), waits=[k])
                k = r.op("V", lambda v: v.tensor_scalar(lnt2_t[:, :], lnt2_t[:, :],
                                                        -0.5, 1.5, AT.mult, AT.add),
                         waits=[k])
                k = r.op("V", lambda v: v.tensor_tensor(lnr_t[:, :], lnr_t[:, :],
                                                        lnt2_t[:, :], AT.mult), waits=[k])
            return k

        def transpose_split(r, src_tile, dst_hi, dst_lo, col_base, t_src, bank_pair,
                            bank_guard):
            last = None
            for half in range(4):
                bkt = bank_pair[half % 2]
                tw = [t_src, bank_guard.get(bkt), c_id]
                mmtok = None
                for jj in range(4):
                    dc = half * 4 + jj
                    mmtok = r.op("T", lambda t, bkt=bkt, jj=jj, dc=dc: t.transpose(
                        pb[bkt][:, jj * 128:(jj + 1) * 128],
                        src_tile[:, dc * 128:(dc + 1) * 128], sident[:, :]),
                        waits=tw if jj == 0 else [], mark=(jj == 3))
                hi = r.op("S", lambda s, bkt=bkt, half=half: s.copy(
                    dap(dst_hi, (half * 4) * TPC + col_base,
                        [[NDC * TPC, 128], [TPC, 4], [1, 128]]),
                    pb[bkt][:, 0:512].rearrange("p (a b) -> p a b", a=4)),
                    waits=[mmtok])
                lo = r.op("V", lambda v, bkt=bkt, half=half: v.tensor_tensor(
                    dap(dst_lo, (half * 4) * TPC + col_base,
                        [[NDC * TPC, 128], [TPC, 4], [1, 128]]),
                    pb[bkt][:, 0:512].rearrange("p (a b) -> p a b", a=4),
                    dap(dst_hi, (half * 4) * TPC + col_base,
                        [[NDC * TPC, 128], [TPC, 4], [1, 128]]),
                    AT.subtract), waits=[mmtok, hi])
                bank_guard[bkt] = lo
                last = lo
            return last

        xn_free = [None, None]
        t_xnT = None
        for tt in range(8):
            sl = tt % 2
            ld = r.dma("Y", "qS", xtile[sl][:, :],
                       dap(x_in, tt * 128 * D, [[D, 128], [1, D]]),
                       waits=[xn_free[sl]])
            xs = xtile[sl]
            k = ln_chain(r, (None, lambda b, xs=xs: xs[:, b * 512:(b + 1) * 512]),
                         bnst, bnag, lnt, lnt2, lnr, ld)
            t_xn = r.op("V", lambda v, sl=sl: v.tensor_scalar(
                xn[sl][:, :], xtile[sl][:, :], bnag[:, 0:1], lnr[:, :],
                AT.subtract, AT.mult), waits=[k])
            t_xnT = transpose_split(r, xn[sl], xnT_h, xnT_l, tt * 128, t_xn,
                                    (6, 7), bank_owner)
            xn_free[sl] = t_xnT


        if dbg_xnt is not None:
            r.dma("G", "qG", dbg_xnt[0].ap(), xnT_h[:, :], waits=[t_xnT])
            r.dma("G", "qG", dbg_xnt[1].ap(), xnT_l[:, :], waits=[t_xnT])

        stLN.close()
        wqb_h = [SB(f"wqb_h{i}", [128, NDC * 128], BF16, stk=stA) for i in range(2)]
        wqb_l = [SB(f"wqb_l{i}", [128, NDC * 128], BF16, stk=stA) for i in range(2)]
        wraw = [SB(f"wraw{i}", [128, NDC * 128], stk=stA) for i in range(2)]
        wraw_free = [None, None]
        vraw = SB("vraw", [128, NDC * 256], stk=stA)
        vraw_free = None
        ev32 = [SB(f"ev32_{i}", [128, 512], stk=stA) for i in range(2)]
        evh = [SB(f"evh_{i}", [128, 512], BF16, stk=stA) for i in range(2)]
        evl = [SB(f"evl_{i}", [128, 512], BF16, stk=stA) for i in range(2)]
        vwh = SB("vwh", [128, NDC * 512], BF16, stk=stA)
        vwl = SB("vwl", [128, NDC * 512], BF16, stk=stA)

        ev_free = [None, None]
        kv_st = [None]
        q_st = [None]
        evi_ctr = [0]

        def evac(r, bank_tok, bank, dst_offs, store_list):
            ev_i = evi_ctr[0]
            evi_ctr[0] = (ev_i + 1) % 2
            t32 = r.op("S", lambda s, bank=bank, ev_i=ev_i:
                       s.copy(ev32[ev_i][:, :], bank[:, :]),
                       waits=[bank_tok, ev_free[ev_i]])
            th = r.op("V", lambda v, ev_i=ev_i: v.tensor_copy(
                evh[ev_i][:, :], ev32[ev_i][:, :]), waits=[t32])
            tl = r.op("V", lambda v, ev_i=ev_i: v.tensor_tensor(
                evl[ev_i][:, :], ev32[ev_i][:, :], evh[ev_i][:, :], AT.subtract),
                waits=[th])
            toks = []
            for (tgt, off_h, off_l, pat, c0, c1) in dst_offs:
                toks.append(r.dma("G", "qG", dap(tgt, off_h, pat),
                                  evh[ev_i][:, c0:c1], waits=[th]))
                toks.append(r.dma("G", "qG", dap(tgt, off_l, pat),
                                  evl[ev_i][:, c0:c1], waits=[tl]))
                store_list[0] = toks[-1]
            ev_free[ev_i] = toks
            return t32

        wslot_free = [None, None]

        quant_gen_holder = [None]

        def qdrive(n):
            g = quant_gen_holder[0]
            if g is None:
                return
            for _ in range(n):
                try:
                    next(g)
                except StopIteration:
                    quant_gen_holder[0] = None
                    return

        def proj_fm(r, m, dst, per, store_list):
            split_tok = {}

            def rec_split(ec):
                sl = ec % 2
                lr = r.dma("Y", "qS", wraw[sl][:, :],
                           dap(wqkvo, m * D * D + ec * 128,
                               [[D, 128], [128 * D, NDC], [1, 128]]),
                           waits=[wraw_free[sl], t_xnT])
                th_ = r.op("S", lambda s_, sl=sl: s_.copy(wqb_h[sl][:, :],
                                                          wraw[sl][:, :]),
                           waits=[lr, wslot_free[sl]])
                ll = r.op("V", lambda v, sl=sl: v.tensor_tensor(
                    wqb_l[sl][:, :], wraw[sl][:, :], wqb_h[sl][:, :], AT.subtract),
                    waits=[th_])
                wraw_free[sl] = ll
                split_tok[ec] = ll

            rec_split(0)
            rec_split(1)
            for ec in range(NDC):
                b0, b1 = 2 * (ec % 2), 2 * (ec % 2) + 1
                sl = ec % 2
                ll = split_tok.pop(ec)
                lh = ll
                mm_last = {}
                for dc in range(NDC):
                    wh = wqb_h[sl][:, dc * 128:(dc + 1) * 128]
                    wl = wqb_l[sl][:, dc * 128:(dc + 1) * 128]
                    mmw = [lh, ll, t_xnT]
                    if dc == 0:
                        mmw += [bank_owner.get(b0), bank_owner.get(b1)]
                    for tg, bk in ((0, b0), (1, b1)):
                        xh = xnT_h[:, dc * TPC + tg * 512:dc * TPC + tg * 512 + 512]
                        xl = xnT_l[:, dc * TPC + tg * 512:dc * TPC + tg * 512 + 512]
                        r.op("T", lambda t, wh=wh, bk=bk, xh=xh, dc=dc: t.matmul(
                            pb[bk][:, :], wh, xh,
                            start=(dc == 0), stop=False),
                            waits=mmw if tg == 0 else [], mark=False)
                        r.op("T", lambda t, wh=wh, bk=bk, xl=xl: t.matmul(
                            pb[bk][:, :], wh, xl,
                            start=False, stop=False), mark=False)
                        mm_last[tg] = r.op(
                            "T", lambda t, wl=wl, bk=bk, xh=xh, dc=dc: t.matmul(
                                pb[bk][:, :], wl, xh,
                                start=False, stop=(dc == NDC - 1)))
                wslot_free[sl] = mm_last[1]
                if ec + 2 < NDC:
                    rec_split(ec + 2)
                for tg, bk in ((0, b0), (1, b1)):
                    rank = ec // 2
                    sec0 = 0
                    off_h = (rank * per + sec0 * SEC
                             + ((ec % 2) * 128) * 1024 + tg * 512)
                    off_l = (rank * per + (sec0 + 1) * SEC
                             + ((ec % 2) * 128) * 1024 + tg * 512)
                    t32 = evac(r, mm_last[tg], pb[bk],
                               [(dst, off_h, off_l, [[1024, 128], [1, 512]], 0, 512)],
                               store_list)
                    bank_owner[bk] = t32
                qdrive(16)

        quant_gen_holder[0] = run_prep_jobs(quant_jobs, pf_in, gen=True,
                                            wbuf=qwbuf, obuf_h=qobuf,
                                            obuf_l=qobuf)
        proj_fm(r, 1, kv_in, KV_PER, kv_st)

        prev_v_mm = None
        prev_v_mm_mid = None
        for eg in range(4):
            vw_tok = {}
            for dch in range(2):
                lr = r.dma("Y", "qS",
                           dap(vraw, 0, [[NDC * 256, 128], [512, 8], [1, 512]]),
                           dap(wqkvo, 2 * D * D + (dch * 8 * 128) * D + eg * 512,
                               [[D, 128], [128 * D, 8], [1, 512]]),
                           waits=[(prev_v_mm_mid if dch == 0 else prev_v_mm),
                                  vraw_free, t_xnT])
                vh_ap = vwh[:, dch * 4096:(dch + 1) * 4096]
                vl_ap = vwl[:, dch * 4096:(dch + 1) * 4096]
                vr_ap = vraw[:, :]
                th_ = r.op("S", lambda s_, vh_ap=vh_ap, vr_ap=vr_ap:
                           s_.copy(vh_ap, vr_ap), waits=[lr])
                vw_tok[dch] = r.op("V", lambda v, vl_ap=vl_ap, vr_ap=vr_ap, vh_ap=vh_ap:
                                   v.tensor_tensor(vl_ap, vr_ap, vh_ap, AT.subtract),
                                   waits=[th_])
                vraw_free = vw_tok[dch]
            ld_last = vw_tok
            for tc in range(8):
                bk = 4 + tc % 2
                mml = None
                for dc in range(NDC):
                    xh = xnT_h[:, dc * TPC + tc * 128:dc * TPC + tc * 128 + 128]
                    xl = xnT_l[:, dc * TPC + tc * 128:dc * TPC + tc * 128 + 128]
                    vh = vwh[:, dc * 512:(dc + 1) * 512]
                    vl = vwl[:, dc * 512:(dc + 1) * 512]
                    mmw = ([ld_last[0], t_xnT, bank_owner.get(bk)] if dc == 0
                           else ([ld_last[1]] if dc == 8 else []))
                    r.op("T", lambda t, bk=bk, xh=xh, vh=vh, dc=dc: t.matmul(
                        pb[bk][:, :], xh, vh, start=(dc == 0), stop=False),
                        waits=mmw, mark=False)
                    r.op("T", lambda t, bk=bk, xh=xh, vl=vl: t.matmul(
                        pb[bk][:, :], xh, vl, start=False, stop=False), mark=False)
                    mml = r.op("T", lambda t, bk=bk, xl=xl, vh=vh, dc=dc: t.matmul(
                        pb[bk][:, :], xl, vh, start=False, stop=(dc == NDC - 1)))
                    if dc == 7 and tc == 7:
                        prev_v_mm_mid = mml
                prev_v_mm = mml
                dsts = []
                for half in range(2):
                    rank = 2 * eg + half
                    off_h = rank * KV_PER + 2 * SEC + (tc * 128) * 256
                    off_l = rank * KV_PER + 3 * SEC + (tc * 128) * 256
                    dsts.append((kv_in, off_h, off_l, [[256, 128], [1, 256]],
                                 half * 256, half * 256 + 256))
                bank_owner[bk] = evac(r, mml, pb[bk], dsts, kv_st)
                qdrive(4)

        t_a2a_kv = r.op("G", lambda g: g.collective_compute(
            "AllToAll", AT.bypass, replica_groups=[list(range(NCORES))],
            ins=[dap(kv_in, 0, [[1, NCORES * KV_PER]]).opt()
                 if False else kv_in.ap().opt()],
            outs=[kv_out.ap().opt()]),
            waits=r.qbar("qG"), ctr="cc")

        proj_fm(r, 0, q_in, Q_PER, q_st)
        t_a2a_q = r.op("G", lambda g: g.collective_compute(
            "AllToAll", AT.bypass, replica_groups=[list(range(NCORES))],
            ins=[q_in.ap().opt()], outs=[q_out.ap().opt()]),
            waits=r.qbar("qG"), ctr="cc")
        if quant_gen_holder[0] is not None:
            for _ in quant_gen_holder[0]:
                pass
            quant_gen_holder[0] = None
        t_agf = r.op("G", lambda g: g.collective_compute(
            "AllGather", AT.bypass, replica_groups=[list(range(NCORES))],
            ins=[pf_in.ap().opt()], outs=[pf_out.ap().opt()]),
            waits=r.qbar("qG"), ctr="cc")

        r.emit_block(nc, sems)
        stA.close()
        stB = ExitStack()

        smask = SB("smask", [128, 4 * 512], stk=stB)
        c_mask = r.dma("Y", "qS", dap(smask, 0, [[2048, 128], [512, 4], [1, 512]]),
                       dap(mask_in, 0, [[512, 128], [128 * 512, 4], [1, 512]]))
        qth = [SB(f"qth{i}", [128, 2 * 512], BF16, stk=stB) for i in range(2)]
        qtl = [SB(f"qtl{i}", [128, 2 * 512], BF16, stk=stB) for i in range(2)]
        kbh = [SB(f"kbh{i}", [128, 2 * 1024], BF16, stk=stB) for i in range(2)]
        kbl = [SB(f"kbl{i}", [128, 2 * 1024], BF16, stk=stB) for i in range(2)]
        vbh = [SB(f"vbh{i}", [128, 8 * 256], BF16, stk=stB) for i in range(2)]
        vbl = [SB(f"vbl{i}", [128, 8 * 256], BF16, stk=stB) for i in range(2)]
        p32 = [SB(f"p32_{i}", [128, 512], stk=stB) for i in range(3)]
        phb = [SB(f"phb{i}", [128, 512], BF16, stk=stB) for i in range(3)]
        plb = [SB(f"plb{i}", [128, 512], BF16, stk=stB) for i in range(3)]
        gate_sb = SB("gate_sb", [128, 512], stk=stB)
        rsum = SB("rsum", [1, 512], stk=stB)
        ao32 = SB("ao32", [128, 512], stk=stB)
        aoh = SB("aoh", [128, 512], BF16, stk=stB)
        aol = SB("aol", [128, 512], BF16, stk=stB)

        ao_last = None
        q_free = [None, None]
        kv_free = [None, None]
        s_free = [bank_owner.get(0), bank_owner.get(1), bank_owner.get(2)]
        p_free = [None, None, None]
        av_free = [bank_owner.get(3), bank_owner.get(4)]
        sums_free = [bank_owner.get(5)]
        gate_bank_free = bank_owner.get(6)
        bcast_bank_free = bank_owner.get(7)
        t_a1_prev = None
        t_bc_prev = None
        ao_buf_free = None

        uidx = 0
        pend_av = None
        t_a2a_aoA = None
        ao_lastA = None
        for cit, ci in enumerate(list(range(0, 16, 2)) + list(range(1, 16, 2))):
            nkb = 4 * (ci + 1)
            src = ci // 2
            cb = (ci % 2) * 512
            qp = cit % 2
            qh_ld = r.dma("Y", "qS", dap(qth[qp], 0, [[1024, 128], [512, 2], [1, 512]]),
                          dap(q_out, src * Q_PER + 0 * SEC + cb,
                              [[1024, 128], [128 * 1024, 2], [1, 512]]),
                          waits=[t_a2a_q, q_free[qp]])
            ql_ld = r.dma("Y", "qS", dap(qtl[qp], 0, [[1024, 128], [512, 2], [1, 512]]),
                          dap(q_out, src * Q_PER + 1 * SEC + cb,
                              [[1024, 128], [128 * 1024, 2], [1, 512]]),
                          waits=[t_a2a_q, q_free[qp]])
            for qd in range((nkb + 7) // 8):
                sl = qd % 2
                klds = []
                for part, buf in ((0, kbh[sl]), (1, kbl[sl])):
                    klds.append(r.dma(
                        "Y", "qS", dap(buf, 0, [[2048, 128], [1024, 2], [1, 1024]]),
                        dap(kv_out, qd * KV_PER + part * SEC,
                            [[1024, 128], [128 * 1024, 2], [1, 1024]]),
                        waits=[t_a2a_kv, kv_free[sl]]))
                for part, buf in ((0, vbh[sl]), (1, vbl[sl])):
                    klds.append(r.dma(
                        "Y", "qS", dap(buf, 0, [[2048, 128], [256, 8], [1, 256]]),
                        dap(kv_out, qd * KV_PER + (2 + part) * SEC,
                            [[256, 128], [128 * 256, 8], [1, 256]]),
                        waits=[t_a2a_kv, kv_free[sl]]))
                nkb_q = min(8, nkb - 8 * qd)
                for hl in range(2):
                    for kbloc in range(nkb_q):
                        kb = 8 * qd + kbloc
                        sb_ = uidx % 3
                        khap = dap(kbh[sl], hl * 1024 + kbloc * 128,
                                   [[2048, 128], [1, 128]])
                        klap = dap(kbl[sl], hl * 1024 + kbloc * 128,
                                   [[2048, 128], [1, 128]])
                        qhap = dap(qth[qp], hl * 512, [[1024, 128], [1, 512]])
                        qlap = dap(qtl[qp], hl * 512, [[1024, 128], [1, 512]])
                        sw = [klds, qh_ld, ql_ld, s_free[sb_]]
                        r.op("T", lambda t, sb_=sb_, khap=khap, qhap=qhap: t.matmul(
                            pb[sb_][:, :], khap, qhap, start=True, stop=False),
                            waits=sw, mark=False)
                        r.op("T", lambda t, sb_=sb_, khap=khap, qlap=qlap: t.matmul(
                            pb[sb_][:, :], khap, qlap, start=False, stop=False),
                            mark=False)
                        t_s = r.op("T", lambda t, sb_=sb_, klap=klap, qhap=qhap:
                                   t.matmul(pb[sb_][:, :], klap, qhap,
                                            start=False, stop=True))
                        t_e = r.op("S", lambda s, sb_=sb_: s.activation(
                            p32[sb_][:, :], pb[sb_][:, :], AF.Exp, scale=SCALE),
                            waits=[t_s, p_free[sb_]])
                        s_free[sb_] = t_e
                        t_pm = t_e
                        if kb >= nkb - 4:
                            o = kb - (nkb - 4)
                            t_pm = r.op("V", lambda v, sb_=sb_, o=o: v.tensor_tensor(
                                p32[sb_][:, :], p32[sb_][:, :],
                                smask[:, o * 512:(o + 1) * 512], AT.mult),
                                waits=[t_e, c_mask])
                        t_ph = r.op("V", lambda v, sb_=sb_: v.tensor_copy(
                            phb[sb_][:, :], p32[sb_][:, :]), waits=[t_pm])
                        t_pl = r.op("V", lambda v, sb_=sb_: v.tensor_tensor(
                            plb[sb_][:, :], p32[sb_][:, :], phb[sb_][:, :],
                            AT.subtract), waits=[t_ph])

                        def mk_av(hl=hl, sl=sl, kbloc=kbloc, kb=kb, nkb=nkb, sb_=sb_,
                                  t_ph=t_ph, t_pl=t_pl):
                            vhap = dap(vbh[sl], kbloc * 256 + hl * 128,
                                       [[2048, 128], [1, 128]])
                            vlap = dap(vbl[sl], kbloc * 256 + hl * 128,
                                       [[2048, 128], [1, 128]])
                            avw = [t_ph, t_pl, t_ones_b]
                            if kb == 0:
                                avw += [av_free[hl], sums_free[0]]
                            r.op("T", lambda t: t.matmul(
                                pb[3 + hl][:, :], vhap, phb[sb_][:, :],
                                start=(kb == 0), stop=False), waits=avw, mark=False)
                            r.op("T", lambda t: t.matmul(
                                pb[3 + hl][:, :], vhap, plb[sb_][:, :],
                                start=False, stop=False), mark=False)
                            r.op("T", lambda t: t.matmul(
                                pb[3 + hl][:, :], vlap, phb[sb_][:, :],
                                start=False, stop=(kb == nkb - 1)), mark=False)
                            r.op("T", lambda t: t.matmul(
                                pb[5][64 * hl + 0:64 * hl + 1, :], sones_b[:, :],
                                phb[sb_][:, :], start=(kb == 0), stop=(kb == nkb - 1),
                                tile_position=(0, 64 * hl + 0)), mark=False)
                            t_u2 = r.op("T", lambda t: t.matmul(
                                pb[5][64 * hl + 32:64 * hl + 33, :], sones_b[:, :],
                                plb[sb_][:, :], start=(kb == 0), stop=(kb == nkb - 1),
                                tile_position=(0, 64 * hl + 32)))
                            p_free[sb_] = t_u2
                            return t_u2

                        if pend_av is not None:
                            tok = pend_av()
                            kv_free[pend_sl] = tok
                        pend_av = mk_av
                        pend_sl = sl
                        uidx += 1
            t_g = {}
            for hl in range(2):
                qhap = dap(qth[qp], hl * 512, [[1024, 128], [1, 512]])
                qlap = dap(qtl[qp], hl * 512, [[1024, 128], [1, 512]])
                gw_ = [t_gwh, t_gwl, gate_bank_free]
                r.op("T", lambda t, qhap=qhap: t.matmul(
                    pb[6][:, (hl) * 0:512] if False else pb[6][:, :],
                    sgwt_h[:, :], qhap, start=True, stop=False),
                    waits=gw_, mark=False)
                r.op("T", lambda t, qlap=qlap: t.matmul(
                    pb[6][:, :], sgwt_h[:, :], qlap, start=False, stop=False),
                    mark=False)
                t_g[hl] = r.op("T", lambda t, qhap=qhap: t.matmul(
                    pb[6][:, :], sgwt_l[:, :], qhap, start=False, stop=True))
                if hl == 0:
                    tok = pend_av()
                    kv_free[pend_sl] = tok
                    pend_av = None
                    last_unit = tok
                t_sg = r.op("S", lambda s: s.activation(
                    gate_sb[:, :], pb[6][:, :], AF.Sigmoid, bias=sgb[:, :],
                    scale=1.0), waits=[t_g[hl], c_gb, t_a1_prev])
                gate_bank_free = t_sg
                t_ra0 = r.op("S", lambda s_, hl=hl: s_.copy(
                    rsum[:, :], pb[5][64 * hl:64 * hl + 1, :]),
                    waits=[last_unit, t_bc_prev])
                t_ra = r.op("V", lambda v, hl=hl: v.tensor_tensor(
                    rsum[:, :], rsum[:, :],
                    pb[5][64 * hl + 32:64 * hl + 33, :], AT.add),
                    waits=[t_ra0])
                t_rc = r.op("V", lambda v: v.reciprocal(
                    rsum[:, :], rsum[:, :]), waits=[t_ra])
                sums_free[0] = t_rc
                t_bc = r.op("T", lambda t: t.matmul(
                    pb[7][:, :], sones_f[:1, :], rsum[:1, :], start=True, stop=True),
                    waits=[t_rc, t_ones_f, bcast_bank_free])
                t_bc_prev = t_bc
                t_a1 = r.op("V", lambda v, hl=hl: v.tensor_tensor(
                    ao32[:, :], pb[3 + hl][:, :], gate_sb[:, :], AT.mult),
                    waits=[t_sg, last_unit, ao_buf_free])
                av_free[hl] = t_a1
                t_a1_prev = t_a1
                t_a2 = r.op("V", lambda v: v.tensor_tensor(
                    ao32[:, :], ao32[:, :], pb[7][:, :], AT.mult),
                    waits=[t_a1, t_bc])
                bcast_bank_free = t_a2
                t_h2 = r.op("S", lambda s: s.copy(aoh[:, :], ao32[:, :]),
                            waits=[t_a2])
                t_l2 = r.op("V", lambda v: v.tensor_tensor(
                    aol[:, :], ao32[:, :], aoh[:, :], AT.subtract), waits=[t_h2])
                dst = ci // 2
                ao_t = ao_inA if ci % 2 == 0 else ao_inB
                off_h = dst * AO2 + 0 * 256 * 512 + (hl * 128) * 512
                off_l = dst * AO2 + 1 * 256 * 512 + (hl * 128) * 512
                s_h = r.dma("G", "qG", dap(ao_t, off_h, [[512, 128], [1, 512]]),
                            aoh[:, :], waits=[t_h2])
                s_l = r.dma("G", "qG", dap(ao_t, off_l, [[512, 128], [1, 512]]),
                            aol[:, :], waits=[t_l2])
                ao_last = s_l
                ao_buf_free = [s_h, s_l]
            q_free[qp] = t_g[1]
            if cit == 7:
                t_a2a_aoA = r.op("G", lambda g: g.collective_compute(
                    "AllToAll", AT.bypass, replica_groups=[list(range(NCORES))],
                    ins=[ao_inA.ap().opt()], outs=[ao_outA.ap().opt()]),
                    waits=r.qbar("qG"), ctr="cc")

        t_a2a_aoB = r.op("G", lambda g: g.collective_compute(
            "AllToAll", AT.bypass, replica_groups=[list(range(NCORES))],
            ins=[ao_inB.ap().opt()], outs=[ao_outB.ap().opt()]),
            waits=r.qbar("qG"), ctr="cc")

        def ow_load(eg, first_wait, owraw, end_wait=None):
            toks = {}
            last = None
            for dch in range(2):
                lr = r.dma("Y", "qS",
                           dap(owraw, 0, [[NDC * 256, 128], [512, 8], [1, 512]]),
                           dap(wqkvo, 3 * D * D + (dch * 8 * 128) * D + eg * 512,
                               [[D, 128], [128 * D, 8], [1, 512]]),
                           waits=(first_wait if dch == 0 else [end_wait]) + [last])
                oh_ap = owh[:, dch * 4096:(dch + 1) * 4096]
                ol_ap = owl[:, dch * 4096:(dch + 1) * 4096]
                or_ap = owraw[:, :]
                th_ = r.op("S", lambda s_, oh_ap=oh_ap, or_ap=or_ap:
                           s_.copy(oh_ap, or_ap), waits=[lr])
                last = r.op("V", lambda v, ol_ap=ol_ap, or_ap=or_ap, oh_ap=oh_ap:
                            v.tensor_tensor(ol_ap, or_ap, oh_ap, AT.subtract),
                            waits=[th_])
                toks[dch] = last
            return toks
        owraw_B = SB("owraw_B", [128, NDC * 256], stk=stB)
        ow0_last = ow_load(0, [], owraw_B)

        r.emit_block(nc, sems)
        stB.close()
        stAB.close()
        stC = ExitStack()

        ach = [SB(f"ach{i}", [128, NDC * 128], BF16, stk=stC) for i in range(2)]
        acl = [SB(f"acl{i}", [128, NDC * 128], BF16, stk=stC) for i in range(2)]
        xsl = [SB(f"xsl{i}", [128, 512], stk=stC) for i in range(2)]
        owraw_C = SB("owraw_C", [128, NDC * 256], stk=stC)
        bnst2 = SB("bnst2", [128, 24], stk=stC)
        bnag2 = SB("bnag2", [128, 2], stk=stC)
        ln2t = SB("ln2t", [128, 1], stk=stC); ln2t2 = SB("ln2t2", [128, 1], stk=stC)
        ln2r = SB("ln2r", [128, 1], stk=stC)
        xn2 = SB("xn2", [128, D], stk=stC)

        x2buf = SB("bigA", [128, 8 * D], stk=stC)
        bank3 = {0: s_free[0], 1: s_free[1], 2: s_free[2], 3: av_free[0]}
        tp2 = {4: av_free[1], 5: sums_free[0]}
        ow_prev_mm = None
        ach_free = [None, None]
        xsl_free = [None, None]
        x2_toks = {}
        xn2T_last = None
        x2sp_st = []
        for tchalf in range(2):
          for eg in range(4):
            if tchalf == 0 and eg == 0:
                ow_lds = ow0_last
            else:
                ow_lds = ow_load(eg, [ow_mid_tok], owraw_C, end_wait=ow_prev_mm)
            for tc in range(tchalf * 4, tchalf * 4 + 4):
                bk = tc % 4
                asl = tc % 2
                ao_src = ao_outA if tc < 4 else ao_outB
                ao_tok = t_a2a_aoA if tc < 4 else t_a2a_aoB
                alds = []
                for part, buf in ((0, ach[asl]), (1, acl[asl])):
                    for rowhalf in range(2):
                        alds.append(r.dma(
                            "Y", "qS",
                            dap(buf, rowhalf * 128, [[NDC * 128, 128], [256, 8], [1, 128]]),
                            dap(ao_src, part * 256 * 512 + rowhalf * 128 * 512
                                + (tc % 4) * 128,
                                [[512, 128], [AO2, 8], [1, 128]]),
                            waits=[ao_tok, ach_free[asl]]))
                mml = None
                for dc in range(NDC):
                    mmw = []
                    if dc == 0:
                        mmw = [alds, ow_lds[0], bank3.get(bk)]
                    elif dc == 8:
                        mmw = [ow_lds[1]]
                    oh = owh[:, dc * 512:(dc + 1) * 512]
                    ol = owl[:, dc * 512:(dc + 1) * 512]
                    ah_ = ach[asl][:, dc * 128:(dc + 1) * 128]
                    al_ = acl[asl][:, dc * 128:(dc + 1) * 128]
                    r.op("T", lambda t, ah_=ah_, bk=bk, oh=oh, dc=dc: t.matmul(
                        pb[bk][:, :], ah_, oh, start=(dc == 0),
                        stop=False), waits=mmw, mark=False)
                    r.op("T", lambda t, ah_=ah_, bk=bk, ol=ol: t.matmul(
                        pb[bk][:, :], ah_, ol, start=False, stop=False),
                        mark=False)
                    mml = r.op("T", lambda t, al_=al_, bk=bk, oh=oh, dc=dc: t.matmul(
                        pb[bk][:, :], al_, oh, start=False,
                        stop=(dc == NDC - 1)))
                    if dc == 7:
                        ow_mid_tok = mml
                ach_free[asl] = mml
                xs = tc % 2
                xld = r.dma("Y", "qS", xsl[xs][:, :],
                            dap(x_in, tc * 128 * D + eg * 512, [[D, 128], [1, 512]]),
                            waits=[xsl_free[xs]])
                t_x2 = r.op("V", lambda v, bk=bk, xs=xs, tc=tc, eg=eg: v.tensor_tensor(
                    x2buf[:, tc * D + eg * 512:tc * D + eg * 512 + 512],
                    pb[bk][:, :], xsl[xs][:, :], AT.add), waits=[mml, xld])
                bank3[bk] = t_x2
                xsl_free[xs] = t_x2
                x2_toks[(tc, eg)] = t_x2
            ow_prev_mm = mml

        for tc in range(8):
            w0 = [x2_toks[(tc, eg)] for eg in range(4)]
            sp = r.dma("G", "qG", dap(x2_sp, tc * 128 * D, [[D, 128], [1, D]]),
                       x2buf[:, tc * D:(tc + 1) * D], waits=w0)
            x2sp_st.append(sp)
            xb = x2buf
            k = ln_chain(r, (None, lambda b, tc=tc, xb=xb:
                             xb[:, tc * D + b * 512:tc * D + (b + 1) * 512]),
                         bnst2, bnag2, ln2t, ln2t2, ln2r, w0[-1])
            for t0 in w0:
                pass
            t_xn2 = r.op("V", lambda v, tc=tc: v.tensor_scalar(
                xn2[:, :], x2buf[:, tc * D:(tc + 1) * D], bnag2[:, 0:1],
                ln2r[:, :], AT.subtract, AT.mult), waits=[k] + w0)
            xn2T_last = transpose_split(r, xn2, xnT_h, xnT_l, tc * 128, t_xn2,
                                        (4, 5), tp2)

        r.emit_block(nc, sems)
        stC.close()
        stBC.close()
        stD = ExitStack()

        for _eng, _nopfn in (("T", lambda t: t.nop()), ("V", lambda v: v.engine_nop()),
                             ("S", lambda s_: s_.nop()), ("G", lambda g: g.engine_nop()),
                             ("Y", lambda y: y.nop())):
            r.op(_eng, _nopfn, waits=list(x2sp_st), mark=False)
        sds_row = SB("sds_row", [1, D], stk=stD)
        c_dsr = r.dma("Y", "qS", sds_row[:, :], ds_in.ap())
        sds_b = SB("sds_b", [128, 512], stk=stD)
        t_dsb_cur = None
        dsb_hg = [-1]

        def ensure_dsb(hg):
            nonlocal t_dsb_cur
            if dsb_hg[0] == hg:
                return t_dsb_cur
            mmw = [c_dsr, t_ones_f, bank_f.get(4)]
            if t_dsb_cur is not None:
                mmw.append(t_dsb_cur)
            mm = r.op("T", lambda t, hg=hg: t.matmul(
                pb[4][:, :], sones_f[:1, :],
                sds_row[:1, hg * 512:(hg + 1) * 512], start=True, stop=True),
                waits=mmw)
            t_dsb_cur = r.op("V", lambda v: v.tensor_copy(sds_b[:, :], pb[4][:, :]),
                             waits=[mm])
            bank_f[4] = t_dsb_cur
            dsb_hg[0] = hg
            return t_dsb_cur

        fuse_h = SB("fuse_h", [128, NIC * 512], BF16, stk=stD)
        fuse_l = SB("fuse_l", [128, NIC * 512], BF16, stk=stD)
        gwb = [SB(f"gwb{i}", [128, NDC * 256], F8, stk=stD) for i in range(2)]
        uwb = [SB(f"uwb{i}", [128, NDC * 256], F8, stk=stD) for i in range(1)]
        dwb = [SB(f"dwb{i}", [128, 512], F8, stk=stD) for i in range(2)]
        fsil = SB("fsil", [128, 512], stk=stD)
        fup = SB("fup", [128, 512], stk=stD)
        x2t = [SB(f"x2t{i}", [128, 512], stk=stD) for i in range(2)]
        otile = [fsil, fup]

        out_last = None
        gw_free = [None, None]
        uw_free = None
        dw_free = [None, None]
        fs_free = None
        fu_free = None
        bank_f = {0: bank3[0], 1: bank3[1], 2: bank3[2], 3: bank3[3],
                  4: tp2[4], 5: tp2[5],
                  6: gate_bank_free, 7: bcast_bank_free}
        x2t_free = [None, None]
        ot_free = [None, None]
        prev_down = None
        for grp in range(2):
            fuse_toks = {}
            for ic in range(NIC):
                bkG = 4 + (ic % 2) * 2
                bkU = bkG + 1
                icg = ic // 2
                gsl = icg % 2
                if ic % 2 == 0:
                    rank = ic // 6
                    coli = ((icg * 2) % 6) * 128
                    g1 = r.dma("Y", "qS",
                               dap(gwb[gsl], 0, [[NDC * 256, 128], [256, NDC], [1, 256]]),
                               dap(pf_out, rank * PF_N + 0 * D * 768
                                   + coli, [[768, 128], [128 * 768, NDC], [1, 256]]),
                               waits=[gw_free[gsl], t_agf])
                    g2 = r.dma("Y", "qS",
                               dap(uwb[0], 0, [[NDC * 256, 128], [256, NDC], [1, 256]]),
                               dap(pf_out, rank * PF_N + 1 * D * 768
                                   + coli, [[768, 128], [128 * 768, NDC], [1, 256]]),
                               waits=[uw_free])
                    gw_lds = (g1, g2)
                mmlG = mmlU = None
                for dc in range(NDC):
                    mmw = [gw_lds[0], xn2T_last]
                    if dc == 0:
                        mmw += [bank_f.get(bkG)]
                    gq = gwb[gsl][:, dc * 256 + (ic % 2) * 128:dc * 256 + (ic % 2) * 128 + 128]
                    xh = xnT_h[:, dc * TPC + grp * 512:dc * TPC + grp * 512 + 512]
                    xl = xnT_l[:, dc * TPC + grp * 512:dc * TPC + grp * 512 + 512]
                    r.op("T", lambda t, gq=gq, bkG=bkG, xh=xh, dc=dc: t.matmul(
                        pb[bkG][:, :], gq, xh, start=(dc == 0),
                        stop=False), waits=mmw, mark=False)
                    mmlG = r.op("T", lambda t, gq=gq, bkG=bkG, xl=xl, dc=dc: t.matmul(
                        pb[bkG][:, :], gq, xl, start=False,
                        stop=(dc == NDC - 1)))
                for dc in range(NDC):
                    mmw = [gw_lds[1], xn2T_last]
                    if dc == 0:
                        mmw += [bank_f.get(bkU)]
                    uq = uwb[0][:, dc * 256 + (ic % 2) * 128:dc * 256 + (ic % 2) * 128 + 128]
                    xh = xnT_h[:, dc * TPC + grp * 512:dc * TPC + grp * 512 + 512]
                    xl = xnT_l[:, dc * TPC + grp * 512:dc * TPC + grp * 512 + 512]
                    r.op("T", lambda t, uq=uq, bkU=bkU, xh=xh, dc=dc: t.matmul(
                        pb[bkU][:, :], uq, xh, start=(dc == 0),
                        stop=False), waits=mmw, mark=False)
                    mmlU = r.op("T", lambda t, uq=uq, bkU=bkU, xl=xl, dc=dc: t.matmul(
                        pb[bkU][:, :], uq, xl, start=False,
                        stop=(dc == NDC - 1)))
                if ic % 2 == 1:
                    gw_free[gsl] = mmlG
                    uw_free = mmlU
                t_si = r.op("S", lambda s, bkG=bkG, ic=ic: s.activation(
                    fsil[:, :], pb[bkG][:, :], AF.Silu, scale=sgs[:, ic:ic + 1]),
                    waits=[mmlG, c_gs, fs_free]
                    + (list(ot_free) if ic == 0 else []))
                bank_f[bkG] = t_si
                t_up = r.op("V", lambda v, bkU=bkU, ic=ic: v.tensor_scalar(
                    fup[:, :], pb[bkU][:, :], sus[:, ic:ic + 1], None, AT.mult),
                    waits=[mmlU, c_us, fu_free]
                    + (list(ot_free) if ic == 0 else []))
                bank_f[bkU] = t_up
                t_f = r.op("V", lambda v: v.tensor_tensor(
                    fup[:, :], fup[:, :], fsil[:, :], AT.mult), waits=[t_si, t_up])
                fs_free = t_f
                t_fh = r.op("S", lambda s, ic=ic: s.copy(
                    fuse_h[:, ic * 512:(ic + 1) * 512], fup[:, :]),
                    waits=[t_f, prev_down if ic == 0 else None])
                t_fl = r.op("V", lambda v, ic=ic: v.tensor_tensor(
                    fuse_l[:, ic * 512:(ic + 1) * 512], fup[:, :],
                    fuse_h[:, ic * 512:(ic + 1) * 512], AT.subtract),
                    waits=[t_fh, prev_down if ic == 0 else None])
                fu_free = t_fl
                fuse_toks[ic] = t_fl
            for hg in range(4):
                mml = {tc: None for tc in range(4)}
                for ic in range(NIC):
                    sl = ic % 2
                    off = (2 * hg) * PF_N + SEC_DN + (ic * 128) * 256
                    dld = r.dma("Y", "qS",
                                dap(dwb[sl], 0, [[512, 128], [256, 2], [1, 256]]),
                                dap(pf_out, off, [[256, 128], [PF_N, 2], [1, 256]]),
                                waits=[dw_free[sl], t_agf])
                    for tc in range(4):
                        bk = tc
                        mmw = [dld, fuse_toks[ic]]
                        if ic == 0:
                            mmw.append(bank_f.get(bk))
                        fh = fuse_h[:, ic * 512 + tc * 128:ic * 512 + tc * 128 + 128]
                        fl = fuse_l[:, ic * 512 + tc * 128:ic * 512 + tc * 128 + 128]
                        r.op("T", lambda t, bk=bk, fh=fh, sl=sl, ic=ic: t.matmul(
                            pb[bk][:, :], fh, dwb[sl][:, :], start=(ic == 0),
                            stop=False), waits=mmw, mark=False)
                        mml[tc] = r.op("T", lambda t, bk=bk, fl=fl, sl=sl, ic=ic:
                                       t.matmul(pb[bk][:, :], fl, dwb[sl][:, :],
                                                start=False, stop=(ic == NIC - 1)))
                    dw_free[sl] = mml[3]
                prev_down = mml[3]
                t_dsb_now = ensure_dsb(hg)
                for tc in range(4):
                    xs = tc % 2
                    xld = r.dma("Y", "qS", x2t[xs][:, :],
                                dap(x2_sp, (grp * 512 + tc * 128) * D + hg * 512,
                                    [[D, 128], [1, 512]]),
                                waits=[x2t_free[xs], x2sp_st[grp * 4 + tc]])
                    ots = tc % 2
                    t_d1 = r.op("V", lambda v, tc=tc, ots=ots: v.tensor_tensor(
                        otile[ots][:, :], pb[tc][:, :],
                        sds_b[:, :], AT.mult),
                        waits=[mml[tc], t_dsb_now, ot_free[ots]])
                    bank_f[tc] = t_d1
                    t_d2 = r.op("V", lambda v, xs=xs, ots=ots: v.tensor_tensor(
                        otile[ots][:, :], otile[ots][:, :], x2t[xs][:, :], AT.add),
                        waits=[t_d1, xld])
                    x2t_free[xs] = t_d2
                    so = r.dma("G", "qG",
                               dap(out_t, (grp * 512 + tc * 128) * D + hg * 512,
                                   [[D, 128], [1, 512]]),
                               otile[ots][:, :], waits=[t_d2])
                    ot_free[ots] = so
                    out_last = so

        for tap_t, src_t in dbg_taps:
            out_last = r.dma("G", "qG", tap_t.ap(), src_t.ap(), waits=[out_last])

        r.op("G", lambda g: g.engine_nop(), waits=[out_last], mark=False)

        r.emit_block(nc, sems)
        stD.close()

    return nc


def prepare_in_maps(inputs):
    x = np.ascontiguousarray(np.asarray(inputs["L0_attn_x"], dtype=np.float32)).reshape(S, D)
    qw = np.asarray(inputs["L0_attn_q_w"], np.float32)
    kw = np.asarray(inputs["L0_attn_k_w"], np.float32)
    vw = np.asarray(inputs["L0_attn_v_w"], np.float32)
    ow = np.asarray(inputs["L0_attn_o_w"], np.float32)
    gwt = np.ascontiguousarray(np.asarray(inputs["L0_attn_gate_w"], np.float32).T)
    gb = np.ascontiguousarray(np.asarray(inputs["L0_attn_gate_b"], np.float32).reshape(HD, 1))
    gw_f = np.asarray(inputs["L0_ffn_gate_w"], np.float32)
    uw_f = np.asarray(inputs["L0_ffn_up_w"], np.float32)
    dw_f = np.asarray(inputs["L0_ffn_down_w"], np.float32)
    gs = np.ascontiguousarray(np.asarray(inputs["L0_ffn_gate_scale"], np.float32).reshape(NIC, 128).T)
    us = np.ascontiguousarray(np.asarray(inputs["L0_ffn_up_scale"], np.float32).reshape(NIC, 128).T)
    ds = np.ascontiguousarray(np.asarray(inputs["L0_ffn_down_scale"], np.float32).reshape(1, D))
    ident = np.eye(128, dtype=np.float32)
    k_idx = np.arange(128).reshape(1, 128, 1)
    q_idx = np.arange(512).reshape(1, 1, 512)
    o_idx = np.arange(4).reshape(4, 1, 1)
    masks = ((128 * o_idx + k_idx) <= q_idx).astype(np.float32)

    qkvoT = np.stack([np.ascontiguousarray(w.T) for w in (qw, kw, vw, ow)])
    in_maps = []
    for c in range(NCORES):
        in_maps.append({
            "x": np.ascontiguousarray(x[c * TPC:(c + 1) * TPC]),
            "wqkvo": qkvoT,
            "gu": np.ascontiguousarray(
                np.stack([gw_f[:, c * 768:(c + 1) * 768], uw_f[:, c * 768:(c + 1) * 768]])),
            "dn": np.ascontiguousarray(dw_f[:, c * 256:(c + 1) * 256]),
            "gwt": gwt, "gb": gb, "gs": gs, "us": us, "ds": ds,
            "ident": ident, "masks": masks,
        })
    return in_maps


def kernel(**inputs):
    in_maps = prepare_in_maps(inputs)
    nc = build_program(DBG)
    res = run_bass_kernel_spmd(nc, in_maps, core_ids=list(range(NCORES)))
    out = np.concatenate([np.asarray(r["out"]) for r in res.results], axis=0)
    return out.reshape(1, S, D).astype(np.float32)


# version 29
# speedup vs baseline: 1.0114x; 1.0042x over previous
from contextlib import ExitStack

import numpy as np

import concourse.bass as bass
import concourse.mybir as mybir
from concourse.bass_utils import run_bass_kernel_spmd
from concourse.aot_env import allow_ici_all_to_all

F32 = mybir.dt.float32
BF16 = mybir.dt.bfloat16
F8 = mybir.dt.float8e4
AT = mybir.AluOpType
AF = mybir.ActivationFunctionType

S = 8192
D = 2048
NH = 16
HD = 128
FF = 6144
NCORES = 8
TPC = S // NCORES
NDC = D // 128
NIC = FF // 128
EPS = 1e-5
SCALE = float(np.float32(1.0) / np.sqrt(np.float32(128.0)))

SEC_DN = 2 * D * 768
PF_N = SEC_DN + FF * 256

SEC = 256 * 1024
KV_PER = 4 * SEC
AO_PER = 2 * SEC

DBG = ()


class Rec:

    ENGS = ("T", "V", "S", "G", "Y")

    NSEM = 8

    def __init__(self):
        self.ops = {e: [] for e in self.ENGS}
        self.cnt = {"T": 0, "V": 0, "S": 0, "G": 0, "Y": 0, "cc": 0}
        self.step = {"T": 1, "V": 1, "S": 1, "G": 1, "Y": 1, "cc": 1}
        for q in ("qS", "qG", "qA"):
            self.cnt[q] = 0
            for i in range(self.NSEM):
                self.cnt[f"{q}{i}"] = 0
                self.step[f"{q}{i}"] = 16

    def qbar(self, queue):
        return [(f"{queue}{i}", self.cnt[f"{queue}{i}"] * 16)
                for i in range(self.NSEM) if self.cnt[f"{queue}{i}"] > 0]

    def op(self, eng, fn, waits=(), mark=True, ctr=None):
        ctr = ctr or eng
        flat = []
        for w in waits:
            if w is None:
                continue
            if isinstance(w, list):
                flat.extend(x for x in w if x is not None)
            else:
                flat.append(w)
        waits = flat
        if eng in ("V", "S", "G") and self.cnt[eng] > 0:
            waits.append((eng, self.cnt[eng] * self.step[eng]))
        for (cn, cv) in waits:
            assert cv <= self.cnt[cn] * self.step[cn], (
                f"wait on future token {cn}={cv} (now {self.cnt[cn] * self.step[cn]})")
        tok = None
        if mark:
            self.cnt[ctr] += 1
            tok = (ctr, self.cnt[ctr] * self.step[ctr])
        self.ops[eng].append((waits, fn, ctr if mark else None))
        return tok

    def dma(self, eng, queue, out, in_, waits=()):
        i = self.cnt[queue]
        self.cnt[queue] += 1
        sem = f"{queue}{i % self.NSEM}"
        w = [x for x in waits if x is not None]
        if i >= self.NSEM:
            w.append((sem, 16 * (i // self.NSEM)))

        def fn(e, out=out, in_=in_):
            return e.dma_start(out=out, in_=in_)
        return self.op(eng, fn, waits=w, ctr=sem)

    def emit_block(self, nc, sems):
        with nc.Block() as blk:
            def mk(ename):
                ops = self.ops[ename]

                def body(eh):
                    wm = {}
                    for waits, fn, ctr in ops:
                        for (cn, cv) in waits:
                            if wm.get(cn, 0) < cv:
                                eh.wait_ge(sems[cn], cv)
                                wm[cn] = cv
                        instr = fn(eh)
                        if ctr is not None:
                            instr.then_inc(sems[ctr], self.step[ctr])
                return body

            for ename, reg in (("T", blk.tensor), ("V", blk.vector),
                               ("S", blk.scalar), ("G", blk.gpsimd),
                               ("Y", blk.sync)):
                if self.ops[ename]:
                    reg(mk(ename))
        self.ops = {e: [] for e in self.ENGS}


def dap(t, off, pattern):
    return bass.AP(t, off, pattern)


def build_program(dbg=()):
    with allow_ici_all_to_all():
        return _build_program(dbg)


def _build_program(dbg=()):
    nc = bass.Bass("TRN2", target_bir_lowering=False)

    x_in = nc.dram_tensor("x", [TPC, D], F32, kind="ExternalInput")
    wqkvo = nc.dram_tensor("wqkvo", [4, D, D], F32, kind="ExternalInput")
    gu_in = nc.dram_tensor("gu", [2, D, 768], F32, kind="ExternalInput")
    dn_in = nc.dram_tensor("dn", [FF, 256], F32, kind="ExternalInput")
    gwt_in = nc.dram_tensor("gwt", [HD, HD], F32, kind="ExternalInput")
    gb_in = nc.dram_tensor("gb", [HD, 1], F32, kind="ExternalInput")
    gs_in = nc.dram_tensor("gs", [128, NIC], F32, kind="ExternalInput")
    us_in = nc.dram_tensor("us", [128, NIC], F32, kind="ExternalInput")
    ds_in = nc.dram_tensor("ds", [1, D], F32, kind="ExternalInput")
    id_in = nc.dram_tensor("ident", [128, 128], F32, kind="ExternalInput")
    mask_in = nc.dram_tensor("masks", [4, 128, 512], F32, kind="ExternalInput")
    out_t = nc.dram_tensor("out", [TPC, D], F32, kind="ExternalOutput")

    pf_in = nc.dram_tensor("pf_in", [PF_N], F8)
    pf_out = nc.dram_tensor("pf_out", [NCORES * PF_N], F8)
    kv_in = nc.dram_tensor("kv_in", [NCORES * KV_PER], BF16)
    kv_out = nc.dram_tensor("kv_out", [NCORES * KV_PER], BF16)
    Q_PER = 2 * SEC
    q_in = nc.dram_tensor("q_in", [NCORES * Q_PER], BF16)
    q_out = nc.dram_tensor("q_out", [NCORES * Q_PER], BF16)
    AO2 = 2 * 256 * 512
    ao_inA = nc.dram_tensor("ao_inA", [NCORES * AO2], BF16)
    ao_outA = nc.dram_tensor("ao_outA", [NCORES * AO2], BF16)
    ao_inB = nc.dram_tensor("ao_inB", [NCORES * AO2], BF16)
    ao_outB = nc.dram_tensor("ao_outB", [NCORES * AO2], BF16)
    x2_sp = nc.dram_tensor("x2_sp", [TPC, D], F32)

    dbg_taps = []
    if "kvq" in dbg:
        dbg_taps.append((nc.dram_tensor("dbg_kv", [NCORES * KV_PER], BF16,
                                        kind="ExternalOutput"), kv_out))
        pass
    if "prep" in dbg:
        pass
        dbg_taps.append((nc.dram_tensor("dbg_prepf", [NCORES * PF_N], F8,
                                        kind="ExternalOutput"), pf_out))

    if "pre" in dbg:
        dbg_taps.append((nc.dram_tensor("dbg_kvin", [NCORES * KV_PER], BF16,
                                        kind="ExternalOutput"), kv_in))
        dbg_taps.append((nc.dram_tensor("dbg_qin", [NCORES * Q_PER], BF16,
                                        kind="ExternalOutput"), q_in))
    dbg_xnt = None
    if "xnt" in dbg:
        dbg_xnt = (nc.dram_tensor("dbg_xnth", [128, NDC * TPC], BF16,
                                  kind="ExternalOutput"),
                   nc.dram_tensor("dbg_xntl", [128, NDC * TPC], BF16,
                                  kind="ExternalOutput"))
    if "x2" in dbg:
        dbg_taps.append((nc.dram_tensor("dbg_x2", [TPC, D], F32,
                                        kind="ExternalOutput"), x2_sp))

    with ExitStack() as st:
        sems = {}
        sem_names = ["T", "V", "S", "G", "Y", "cc"]
        for q in ("qS", "qG", "qA"):
            sem_names += [f"{q}{i}" for i in range(Rec.NSEM)]
        for name in sem_names:
            sems[name] = st.enter_context(nc.semaphore("sem_" + name))

        pb = [st.enter_context(nc.psum_tensor(f"pb{i}", [128, 512], F32))
              for i in range(8)]

        def SB(name, shape, dt=F32, stk=None):
            return (stk or st).enter_context(nc.sbuf_tensor(name, shape, dt))

        sident = SB("sident", [128, 128])
        sgwt32 = SB("sgwt32", [128, 128])
        sgwt_h = SB("sgwt_h", [128, 128], BF16)
        sgwt_l = SB("sgwt_l", [128, 128], BF16)
        sgb = SB("sgb", [128, 1])
        sgs = SB("sgs", [128, NIC])
        sus = SB("sus", [128, NIC])
        sones_b = SB("sones_b", [128, 1], BF16)
        sones_f = SB("sones_f", [1, 128])

        xnT_h = SB("xnT_h", [128, NDC * TPC], BF16)
        xnT_l = SB("xnT_l", [128, NDC * TPC], BF16)
        r = Rec()
        stBC = ExitStack()
        owh = SB("owh", [128, NDC * 512], BF16, stk=stBC)
        owl = SB("owl", [128, NDC * 512], BF16, stk=stBC)

        stA = ExitStack()
        stAB = ExitStack()

        c_id = r.dma("Y", "qS", sident[:, :], id_in.ap())
        c_gb = r.dma("Y", "qS", sgb[:, :], gb_in.ap())
        c_gs = r.dma("Y", "qS", sgs[:, :], gs_in.ap())
        c_us = r.dma("Y", "qS", sus[:, :], us_in.ap())
        c_gwt = r.dma("Y", "qS", sgwt32[:, :], gwt_in.ap())
        t_ones_b = r.op("G", lambda g: g.memset(sones_b[:, :], 1.0))
        t_ones_f = r.op("G", lambda g: g.memset(sones_f[:, :], 1.0))

        t_gwh = r.op("V", lambda v: v.tensor_copy(sgwt_h[:, :], sgwt32[:, :]),
                     waits=[c_gwt])
        t_gwl = r.op("V", lambda v: v.tensor_tensor(sgwt_l[:, :], sgwt32[:, :],
                                                    sgwt_h[:, :], AT.subtract),
                     waits=[t_gwh])
        bank_owner = {j: None for j in range(8)}

        t_c = SB("t_c", [128, 768], stk=stAB); t_d = SB("t_d", [128, 768], stk=stAB)
        t_h = SB("t_h", [128, 768], stk=stAB)
        t_m = SB("t_m", [128, 768], mybir.dt.uint8, stk=stAB)
        qwbuf = [SB(f"qwbuf{i}", [128, 768], stk=stAB) for i in range(2)]
        qobuf = [SB(f"qobuf{i}", [128, 768], F8, stk=stAB) for i in range(2)]
        stLN = ExitStack()


        quant_jobs = []
        for m in range(2):
            for dc in range(NDC):
                quant_jobs.append(("quant",
                                   dap(gu_in, (m * D + dc * 128) * 768, [[768, 128], [1, 768]]),
                                   768,
                                   m * D * 768 + dc * 128 * 768, None, 768))
        for icc in range(NIC):
            quant_jobs.append(("quant",
                               dap(dn_in, icc * 128 * 256, [[256, 128], [1, 256]]),
                               256, SEC_DN + icc * 128 * 256, None, 256))

        slot_free = [None, None]
        prep_last = None

        prep_marks = []

        def run_prep_jobs(jobs, tgt, ldeng="Y", ldq="qS", gen=False,
                          wbuf=None, obuf_h=None, obuf_l=None):
          nonlocal prep_last
          for j, (kind, src, wid, off_h, off_l, rstr) in enumerate(jobs):
            prep_in = tgt
            sl = j % 2
            ld = r.dma(ldeng, ldq, wbuf[sl][:, 0:wid], src, waits=[slot_free[sl]])
            if gen:
                yield
            if kind == "split":
                th = r.op("S", lambda s_, sl=sl, wid=wid: s_.copy(
                    obuf_h[sl][:, 0:wid], wbuf[sl][:, 0:wid]), waits=[ld])
                tl = r.op("V", lambda v, sl=sl, wid=wid: v.tensor_tensor(
                    obuf_l[sl][:, 0:wid], wbuf[sl][:, 0:wid],
                    obuf_h[sl][:, 0:wid], AT.subtract), waits=[th])
                r.dma("G", "qG", dap(prep_in, off_h, [[rstr, 128], [1, wid]]),
                      obuf_h[sl][:, 0:wid], waits=[th])
                s2 = r.dma("G", "qG", dap(prep_in, off_l, [[rstr, 128], [1, wid]]),
                           obuf_l[sl][:, 0:wid], waits=[tl])
                slot_free[sl] = s2
                prep_last = s2
                prep_marks.append(s2)
            else:
                k1 = r.op("V", lambda v, sl=sl, wid=wid: v.tensor_scalar(
                    t_c[:, 0:wid], wbuf[sl][:, 0:wid], 1048577.0, None, AT.mult),
                    waits=[ld])
                if gen:
                    yield
                k2 = r.op("V", lambda v, sl=sl, wid=wid: v.tensor_tensor(
                    t_d[:, 0:wid], t_c[:, 0:wid], wbuf[sl][:, 0:wid], AT.subtract),
                    waits=[k1])
                if gen:
                    yield
                k3 = r.op("V", lambda v, wid=wid: v.tensor_tensor(
                    t_h[:, 0:wid], t_c[:, 0:wid], t_d[:, 0:wid], AT.subtract),
                    waits=[k2])
                if gen:
                    yield
                k4 = r.op("V", lambda v, sl=sl, wid=wid: v.tensor_scalar(
                    t_c[:, 0:wid], wbuf[sl][:, 0:wid], 24576.0, -24576.0,
                    AT.add, AT.add), waits=[ld])
                if gen:
                    yield
                k5a = r.op("V", lambda v, sl=sl, wid=wid: v.tensor_tensor(
                    t_d[:, 0:wid], wbuf[sl][:, 0:wid], wbuf[sl][:, 0:wid],
                    AT.mult), waits=[ld, k3])
                if gen:
                    yield
                k5 = r.op("V", lambda v, wid=wid: v.tensor_scalar(
                    t_m[:, 0:wid], t_d[:, 0:wid], 2.0 ** -12, None,
                    AT.is_ge), waits=[k5a])
                if gen:
                    yield
                k6 = r.op("V", lambda v, sl=sl, wid=wid: v.select(
                    obuf_h[sl][:, 0:wid], t_m[:, 0:wid], t_h[:, 0:wid],
                    t_c[:, 0:wid], add_drain=True), waits=[k3, k4, k5])
                if gen:
                    yield
                s1 = r.dma("G", "qG", dap(prep_in, off_h, [[rstr, 128], [1, wid]]),
                           obuf_h[sl][:, 0:wid], waits=[k6])
                slot_free[sl] = s1
                prep_last = s1



        xtile = [SB(f"xtile{i}", [128, D], stk=stLN) for i in range(2)]
        bnst = SB("bnst", [128, 24], stk=stLN)
        bnag = SB("bnag", [128, 2], stk=stLN)
        lnt = SB("lnt", [128, 1], stk=stLN); lnt2 = SB("lnt2", [128, 1], stk=stLN)
        lnr = SB("lnr", [128, 1], stk=stLN)
        xn = [SB(f"xn{i}", [128, D], stk=stLN) for i in range(2)]

        def ln_chain(r, x_ap, bnst_t, bnag_t, lnt_t, lnt2_t, lnr_t, ld_tok):
            k = None
            for b in range(4):
                k = r.op("V", lambda v, b=b: v.bn_stats(
                    bnst_t[:, b * 6:(b + 1) * 6], x_ap[1](b)), waits=[ld_tok])
            k = r.op("V", lambda v: v.bn_aggr(bnag_t[:, :], bnst_t[:, :]), waits=[k])
            k = r.op("V", lambda v: v.tensor_scalar(lnt_t[:, :], bnag_t[:, 1:2], EPS,
                                                    None, AT.add), waits=[k])
            k = r.op("V", lambda v: v.reciprocal(lnt2_t[:, :], lnt_t[:, :]), waits=[k])
            k = r.op("S", lambda s: s.activation(lnr_t[:, :], lnt2_t[:, :], AF.Sqrt),
                     waits=[k])
            for _ in range(2):
                k = r.op("V", lambda v: v.tensor_tensor(lnt2_t[:, :], lnr_t[:, :],
                                                        lnr_t[:, :], AT.mult), waits=[k])
                k = r.op("V", lambda v: v.tensor_tensor(lnt2_t[:, :], lnt2_t[:, :],
                                                        lnt_t[:, :], AT.mult), waits=[k])
                k = r.op("V", lambda v: v.tensor_scalar(lnt2_t[:, :], lnt2_t[:, :],
                                                        -0.5, 1.5, AT.mult, AT.add),
                         waits=[k])
                k = r.op("V", lambda v: v.tensor_tensor(lnr_t[:, :], lnr_t[:, :],
                                                        lnt2_t[:, :], AT.mult), waits=[k])
            return k

        def transpose_split(r, src_tile, dst_hi, dst_lo, col_base, t_src, bank_pair,
                            bank_guard):
            last = None
            for half in range(4):
                bkt = bank_pair[half % 2]
                tw = [t_src, bank_guard.get(bkt), c_id]
                mmtok = None
                for jj in range(4):
                    dc = half * 4 + jj
                    mmtok = r.op("T", lambda t, bkt=bkt, jj=jj, dc=dc: t.transpose(
                        pb[bkt][:, jj * 128:(jj + 1) * 128],
                        src_tile[:, dc * 128:(dc + 1) * 128], sident[:, :]),
                        waits=tw if jj == 0 else [], mark=(jj == 3))
                hi = r.op("S", lambda s, bkt=bkt, half=half: s.copy(
                    dap(dst_hi, (half * 4) * TPC + col_base,
                        [[NDC * TPC, 128], [TPC, 4], [1, 128]]),
                    pb[bkt][:, 0:512].rearrange("p (a b) -> p a b", a=4)),
                    waits=[mmtok])
                lo = r.op("V", lambda v, bkt=bkt, half=half: v.tensor_tensor(
                    dap(dst_lo, (half * 4) * TPC + col_base,
                        [[NDC * TPC, 128], [TPC, 4], [1, 128]]),
                    pb[bkt][:, 0:512].rearrange("p (a b) -> p a b", a=4),
                    dap(dst_hi, (half * 4) * TPC + col_base,
                        [[NDC * TPC, 128], [TPC, 4], [1, 128]]),
                    AT.subtract), waits=[mmtok, hi])
                bank_guard[bkt] = lo
                last = lo
            return last

        xn_free = [None, None]
        t_xnT = None
        for tt in range(8):
            sl = tt % 2
            ld = r.dma("Y", "qS", xtile[sl][:, :],
                       dap(x_in, tt * 128 * D, [[D, 128], [1, D]]),
                       waits=[xn_free[sl]])
            xs = xtile[sl]
            k = ln_chain(r, (None, lambda b, xs=xs: xs[:, b * 512:(b + 1) * 512]),
                         bnst, bnag, lnt, lnt2, lnr, ld)
            t_xn = r.op("V", lambda v, sl=sl: v.tensor_scalar(
                xn[sl][:, :], xtile[sl][:, :], bnag[:, 0:1], lnr[:, :],
                AT.subtract, AT.mult), waits=[k])
            t_xnT = transpose_split(r, xn[sl], xnT_h, xnT_l, tt * 128, t_xn,
                                    (6, 7), bank_owner)
            xn_free[sl] = t_xnT


        if dbg_xnt is not None:
            r.dma("G", "qG", dbg_xnt[0].ap(), xnT_h[:, :], waits=[t_xnT])
            r.dma("G", "qG", dbg_xnt[1].ap(), xnT_l[:, :], waits=[t_xnT])

        stLN.close()
        wqb_h = [SB(f"wqb_h{i}", [128, NDC * 128], BF16, stk=stA) for i in range(2)]
        wqb_l = [SB(f"wqb_l{i}", [128, NDC * 128], BF16, stk=stA) for i in range(2)]
        wraw = [SB(f"wraw{i}", [128, NDC * 128], stk=stA) for i in range(2)]
        wraw_free = [None, None]
        vraw = SB("vraw", [128, NDC * 256], stk=stA)
        vraw_free = None
        ev32 = [SB(f"ev32_{i}", [128, 512], stk=stA) for i in range(2)]
        evh = [SB(f"evh_{i}", [128, 512], BF16, stk=stA) for i in range(2)]
        evl = [SB(f"evl_{i}", [128, 512], BF16, stk=stA) for i in range(2)]
        vwh = SB("vwh", [128, NDC * 512], BF16, stk=stA)
        vwl = SB("vwl", [128, NDC * 512], BF16, stk=stA)

        ev_free = [None, None]
        kv_st = [None]
        q_st = [None]
        evi_ctr = [0]

        def evac(r, bank_tok, bank, dst_offs, store_list):
            ev_i = evi_ctr[0]
            evi_ctr[0] = (ev_i + 1) % 2
            t32 = r.op("S", lambda s, bank=bank, ev_i=ev_i:
                       s.copy(ev32[ev_i][:, :], bank[:, :]),
                       waits=[bank_tok, ev_free[ev_i]])
            th = r.op("V", lambda v, ev_i=ev_i: v.tensor_copy(
                evh[ev_i][:, :], ev32[ev_i][:, :]), waits=[t32])
            tl = r.op("V", lambda v, ev_i=ev_i: v.tensor_tensor(
                evl[ev_i][:, :], ev32[ev_i][:, :], evh[ev_i][:, :], AT.subtract),
                waits=[th])
            toks = []
            for (tgt, off_h, off_l, pat, c0, c1) in dst_offs:
                toks.append(r.dma("G", "qG", dap(tgt, off_h, pat),
                                  evh[ev_i][:, c0:c1], waits=[th]))
                toks.append(r.dma("G", "qG", dap(tgt, off_l, pat),
                                  evl[ev_i][:, c0:c1], waits=[tl]))
                store_list[0] = toks[-1]
            ev_free[ev_i] = toks
            return t32

        wslot_free = [None, None]

        quant_gen_holder = [None]

        def qdrive(n):
            g = quant_gen_holder[0]
            if g is None:
                return
            for _ in range(n):
                try:
                    next(g)
                except StopIteration:
                    quant_gen_holder[0] = None
                    return

        def proj_fm(r, m, dst, per, store_list):
            split_tok = {}

            def rec_split(ec):
                sl = ec % 2
                lr = r.dma("Y", "qS", wraw[sl][:, :],
                           dap(wqkvo, m * D * D + ec * 128,
                               [[D, 128], [128 * D, NDC], [1, 128]]),
                           waits=[wraw_free[sl], t_xnT])
                th_ = r.op("S", lambda s_, sl=sl: s_.copy(wqb_h[sl][:, :],
                                                          wraw[sl][:, :]),
                           waits=[lr, wslot_free[sl]])
                ll = r.op("V", lambda v, sl=sl: v.tensor_tensor(
                    wqb_l[sl][:, :], wraw[sl][:, :], wqb_h[sl][:, :], AT.subtract),
                    waits=[th_])
                wraw_free[sl] = ll
                split_tok[ec] = ll

            rec_split(0)
            rec_split(1)
            for ec in range(NDC):
                b0, b1 = 2 * (ec % 2), 2 * (ec % 2) + 1
                sl = ec % 2
                ll = split_tok.pop(ec)
                lh = ll
                mm_last = {}
                for dc in range(NDC):
                    wh = wqb_h[sl][:, dc * 128:(dc + 1) * 128]
                    wl = wqb_l[sl][:, dc * 128:(dc + 1) * 128]
                    mmw = [lh, ll, t_xnT]
                    if dc == 0:
                        mmw += [bank_owner.get(b0), bank_owner.get(b1)]
                    for tg, bk in ((0, b0), (1, b1)):
                        xh = xnT_h[:, dc * TPC + tg * 512:dc * TPC + tg * 512 + 512]
                        xl = xnT_l[:, dc * TPC + tg * 512:dc * TPC + tg * 512 + 512]
                        r.op("T", lambda t, wh=wh, bk=bk, xh=xh, dc=dc: t.matmul(
                            pb[bk][:, :], wh, xh,
                            start=(dc == 0), stop=False),
                            waits=mmw if tg == 0 else [], mark=False)
                        r.op("T", lambda t, wh=wh, bk=bk, xl=xl: t.matmul(
                            pb[bk][:, :], wh, xl,
                            start=False, stop=False), mark=False)
                        mm_last[tg] = r.op(
                            "T", lambda t, wl=wl, bk=bk, xh=xh, dc=dc: t.matmul(
                                pb[bk][:, :], wl, xh,
                                start=False, stop=(dc == NDC - 1)))
                wslot_free[sl] = mm_last[1]
                if ec + 2 < NDC:
                    rec_split(ec + 2)
                for tg, bk in ((0, b0), (1, b1)):
                    rank = ec // 2
                    sec0 = 0
                    off_h = (rank * per + sec0 * SEC
                             + ((ec % 2) * 128) * 1024 + tg * 512)
                    off_l = (rank * per + (sec0 + 1) * SEC
                             + ((ec % 2) * 128) * 1024 + tg * 512)
                    t32 = evac(r, mm_last[tg], pb[bk],
                               [(dst, off_h, off_l, [[1024, 128], [1, 512]], 0, 512)],
                               store_list)
                    bank_owner[bk] = t32
                qdrive(10)

        quant_gen_holder[0] = run_prep_jobs(quant_jobs, pf_in, gen=True,
                                            wbuf=qwbuf, obuf_h=qobuf,
                                            obuf_l=qobuf)
        proj_fm(r, 1, kv_in, KV_PER, kv_st)

        prev_v_mm = None
        prev_v_mm_mid = None
        for eg in range(4):
            vw_tok = {}
            for dch in range(2):
                lr = r.dma("Y", "qS",
                           dap(vraw, 0, [[NDC * 256, 128], [512, 8], [1, 512]]),
                           dap(wqkvo, 2 * D * D + (dch * 8 * 128) * D + eg * 512,
                               [[D, 128], [128 * D, 8], [1, 512]]),
                           waits=[(prev_v_mm_mid if dch == 0 else prev_v_mm),
                                  vraw_free, t_xnT])
                vh_ap = vwh[:, dch * 4096:(dch + 1) * 4096]
                vl_ap = vwl[:, dch * 4096:(dch + 1) * 4096]
                vr_ap = vraw[:, :]
                th_ = r.op("S", lambda s_, vh_ap=vh_ap, vr_ap=vr_ap:
                           s_.copy(vh_ap, vr_ap), waits=[lr])
                vw_tok[dch] = r.op("V", lambda v, vl_ap=vl_ap, vr_ap=vr_ap, vh_ap=vh_ap:
                                   v.tensor_tensor(vl_ap, vr_ap, vh_ap, AT.subtract),
                                   waits=[th_])
                vraw_free = vw_tok[dch]
            ld_last = vw_tok
            for tc in range(8):
                bk = 4 + tc % 2
                mml = None
                for dc in range(NDC):
                    xh = xnT_h[:, dc * TPC + tc * 128:dc * TPC + tc * 128 + 128]
                    xl = xnT_l[:, dc * TPC + tc * 128:dc * TPC + tc * 128 + 128]
                    vh = vwh[:, dc * 512:(dc + 1) * 512]
                    vl = vwl[:, dc * 512:(dc + 1) * 512]
                    mmw = ([ld_last[0], t_xnT, bank_owner.get(bk)] if dc == 0
                           else ([ld_last[1]] if dc == 8 else []))
                    r.op("T", lambda t, bk=bk, xh=xh, vh=vh, dc=dc: t.matmul(
                        pb[bk][:, :], xh, vh, start=(dc == 0), stop=False),
                        waits=mmw, mark=False)
                    r.op("T", lambda t, bk=bk, xh=xh, vl=vl: t.matmul(
                        pb[bk][:, :], xh, vl, start=False, stop=False), mark=False)
                    mml = r.op("T", lambda t, bk=bk, xl=xl, vh=vh, dc=dc: t.matmul(
                        pb[bk][:, :], xl, vh, start=False, stop=(dc == NDC - 1)))
                    if dc == 7 and tc == 7:
                        prev_v_mm_mid = mml
                prev_v_mm = mml
                dsts = []
                for half in range(2):
                    rank = 2 * eg + half
                    off_h = rank * KV_PER + 2 * SEC + (tc * 128) * 256
                    off_l = rank * KV_PER + 3 * SEC + (tc * 128) * 256
                    dsts.append((kv_in, off_h, off_l, [[256, 128], [1, 256]],
                                 half * 256, half * 256 + 256))
                bank_owner[bk] = evac(r, mml, pb[bk], dsts, kv_st)
                qdrive(10 if eg < 3 else 4)

        t_a2a_kv = r.op("G", lambda g: g.collective_compute(
            "AllToAll", AT.bypass, replica_groups=[list(range(NCORES))],
            ins=[dap(kv_in, 0, [[1, NCORES * KV_PER]]).opt()
                 if False else kv_in.ap().opt()],
            outs=[kv_out.ap().opt()]),
            waits=r.qbar("qG"), ctr="cc")

        proj_fm(r, 0, q_in, Q_PER, q_st)
        t_a2a_q = r.op("G", lambda g: g.collective_compute(
            "AllToAll", AT.bypass, replica_groups=[list(range(NCORES))],
            ins=[q_in.ap().opt()], outs=[q_out.ap().opt()]),
            waits=r.qbar("qG"), ctr="cc")
        if quant_gen_holder[0] is not None:
            for _ in quant_gen_holder[0]:
                pass
            quant_gen_holder[0] = None
        t_agf = r.op("G", lambda g: g.collective_compute(
            "AllGather", AT.bypass, replica_groups=[list(range(NCORES))],
            ins=[pf_in.ap().opt()], outs=[pf_out.ap().opt()]),
            waits=r.qbar("qG"), ctr="cc")

        r.emit_block(nc, sems)
        stA.close()
        stB = ExitStack()

        smask = SB("smask", [128, 4 * 512], stk=stB)
        c_mask = r.dma("Y", "qS", dap(smask, 0, [[2048, 128], [512, 4], [1, 512]]),
                       dap(mask_in, 0, [[512, 128], [128 * 512, 4], [1, 512]]))
        qth = [SB(f"qth{i}", [128, 2 * 512], BF16, stk=stB) for i in range(2)]
        qtl = [SB(f"qtl{i}", [128, 2 * 512], BF16, stk=stB) for i in range(2)]
        kbh = [SB(f"kbh{i}", [128, 2 * 1024], BF16, stk=stB) for i in range(2)]
        kbl = [SB(f"kbl{i}", [128, 2 * 1024], BF16, stk=stB) for i in range(2)]
        vbh = [SB(f"vbh{i}", [128, 8 * 256], BF16, stk=stB) for i in range(2)]
        vbl = [SB(f"vbl{i}", [128, 8 * 256], BF16, stk=stB) for i in range(2)]
        p32 = [SB(f"p32_{i}", [128, 512], stk=stB) for i in range(3)]
        phb = [SB(f"phb{i}", [128, 512], BF16, stk=stB) for i in range(3)]
        plb = [SB(f"plb{i}", [128, 512], BF16, stk=stB) for i in range(3)]
        gate_sb = SB("gate_sb", [128, 512], stk=stB)
        rsum = SB("rsum", [1, 512], stk=stB)
        ao32 = SB("ao32", [128, 512], stk=stB)
        aoh = SB("aoh", [128, 512], BF16, stk=stB)
        aol = SB("aol", [128, 512], BF16, stk=stB)

        ao_last = None
        q_free = [None, None]
        kv_free = [None, None]
        s_free = [bank_owner.get(0), bank_owner.get(1), bank_owner.get(2)]
        p_free = [None, None, None]
        av_free = [bank_owner.get(3), bank_owner.get(4)]
        sums_free = [bank_owner.get(5)]
        gate_bank_free = bank_owner.get(6)
        bcast_bank_free = bank_owner.get(7)
        t_a1_prev = None
        t_bc_prev = None
        ao_buf_free = None

        uidx = 0
        pend_av = None
        t_a2a_aoA = None
        ao_lastA = None
        for cit, ci in enumerate(list(range(0, 16, 2)) + list(range(1, 16, 2))):
            nkb = 4 * (ci + 1)
            src = ci // 2
            cb = (ci % 2) * 512
            qp = cit % 2
            qh_ld = r.dma("Y", "qS", dap(qth[qp], 0, [[1024, 128], [512, 2], [1, 512]]),
                          dap(q_out, src * Q_PER + 0 * SEC + cb,
                              [[1024, 128], [128 * 1024, 2], [1, 512]]),
                          waits=[t_a2a_q, q_free[qp]])
            ql_ld = r.dma("Y", "qS", dap(qtl[qp], 0, [[1024, 128], [512, 2], [1, 512]]),
                          dap(q_out, src * Q_PER + 1 * SEC + cb,
                              [[1024, 128], [128 * 1024, 2], [1, 512]]),
                          waits=[t_a2a_q, q_free[qp]])
            for qd in range((nkb + 7) // 8):
                sl = qd % 2
                klds = []
                for part, buf in ((0, kbh[sl]), (1, kbl[sl])):
                    klds.append(r.dma(
                        "Y", "qS", dap(buf, 0, [[2048, 128], [1024, 2], [1, 1024]]),
                        dap(kv_out, qd * KV_PER + part * SEC,
                            [[1024, 128], [128 * 1024, 2], [1, 1024]]),
                        waits=[t_a2a_kv, kv_free[sl]]))
                for part, buf in ((0, vbh[sl]), (1, vbl[sl])):
                    klds.append(r.dma(
                        "Y", "qS", dap(buf, 0, [[2048, 128], [256, 8], [1, 256]]),
                        dap(kv_out, qd * KV_PER + (2 + part) * SEC,
                            [[256, 128], [128 * 256, 8], [1, 256]]),
                        waits=[t_a2a_kv, kv_free[sl]]))
                nkb_q = min(8, nkb - 8 * qd)
                for hl in range(2):
                    for kbloc in range(nkb_q):
                        kb = 8 * qd + kbloc
                        sb_ = uidx % 3
                        khap = dap(kbh[sl], hl * 1024 + kbloc * 128,
                                   [[2048, 128], [1, 128]])
                        klap = dap(kbl[sl], hl * 1024 + kbloc * 128,
                                   [[2048, 128], [1, 128]])
                        qhap = dap(qth[qp], hl * 512, [[1024, 128], [1, 512]])
                        qlap = dap(qtl[qp], hl * 512, [[1024, 128], [1, 512]])
                        sw = [klds, qh_ld, ql_ld, s_free[sb_]]
                        r.op("T", lambda t, sb_=sb_, khap=khap, qhap=qhap: t.matmul(
                            pb[sb_][:, :], khap, qhap, start=True, stop=False),
                            waits=sw, mark=False)
                        r.op("T", lambda t, sb_=sb_, khap=khap, qlap=qlap: t.matmul(
                            pb[sb_][:, :], khap, qlap, start=False, stop=False),
                            mark=False)
                        t_s = r.op("T", lambda t, sb_=sb_, klap=klap, qhap=qhap:
                                   t.matmul(pb[sb_][:, :], klap, qhap,
                                            start=False, stop=True))
                        t_e = r.op("S", lambda s, sb_=sb_: s.activation(
                            p32[sb_][:, :], pb[sb_][:, :], AF.Exp, scale=SCALE),
                            waits=[t_s, p_free[sb_]])
                        s_free[sb_] = t_e
                        t_pm = t_e
                        if kb >= nkb - 4:
                            o = kb - (nkb - 4)
                            t_pm = r.op("V", lambda v, sb_=sb_, o=o: v.tensor_tensor(
                                p32[sb_][:, :], p32[sb_][:, :],
                                smask[:, o * 512:(o + 1) * 512], AT.mult),
                                waits=[t_e, c_mask])
                        t_ph = r.op("V", lambda v, sb_=sb_: v.tensor_copy(
                            phb[sb_][:, :], p32[sb_][:, :]), waits=[t_pm])
                        t_pl = r.op("V", lambda v, sb_=sb_: v.tensor_tensor(
                            plb[sb_][:, :], p32[sb_][:, :], phb[sb_][:, :],
                            AT.subtract), waits=[t_ph])

                        def mk_av(hl=hl, sl=sl, kbloc=kbloc, kb=kb, nkb=nkb, sb_=sb_,
                                  t_ph=t_ph, t_pl=t_pl):
                            vhap = dap(vbh[sl], kbloc * 256 + hl * 128,
                                       [[2048, 128], [1, 128]])
                            vlap = dap(vbl[sl], kbloc * 256 + hl * 128,
                                       [[2048, 128], [1, 128]])
                            avw = [t_ph, t_pl, t_ones_b]
                            if kb == 0:
                                avw += [av_free[hl], sums_free[0]]
                            r.op("T", lambda t: t.matmul(
                                pb[3 + hl][:, :], vhap, phb[sb_][:, :],
                                start=(kb == 0), stop=False), waits=avw, mark=False)
                            r.op("T", lambda t: t.matmul(
                                pb[3 + hl][:, :], vhap, plb[sb_][:, :],
                                start=False, stop=False), mark=False)
                            r.op("T", lambda t: t.matmul(
                                pb[3 + hl][:, :], vlap, phb[sb_][:, :],
                                start=False, stop=(kb == nkb - 1)), mark=False)
                            r.op("T", lambda t: t.matmul(
                                pb[5][64 * hl + 0:64 * hl + 1, :], sones_b[:, :],
                                phb[sb_][:, :], start=(kb == 0), stop=(kb == nkb - 1),
                                tile_position=(0, 64 * hl + 0)), mark=False)
                            t_u2 = r.op("T", lambda t: t.matmul(
                                pb[5][64 * hl + 32:64 * hl + 33, :], sones_b[:, :],
                                plb[sb_][:, :], start=(kb == 0), stop=(kb == nkb - 1),
                                tile_position=(0, 64 * hl + 32)))
                            p_free[sb_] = t_u2
                            return t_u2

                        if pend_av is not None:
                            tok = pend_av()
                            kv_free[pend_sl] = tok
                        pend_av = mk_av
                        pend_sl = sl
                        uidx += 1
            t_g = {}
            for hl in range(2):
                qhap = dap(qth[qp], hl * 512, [[1024, 128], [1, 512]])
                qlap = dap(qtl[qp], hl * 512, [[1024, 128], [1, 512]])
                gw_ = [t_gwh, t_gwl, gate_bank_free]
                r.op("T", lambda t, qhap=qhap: t.matmul(
                    pb[6][:, (hl) * 0:512] if False else pb[6][:, :],
                    sgwt_h[:, :], qhap, start=True, stop=False),
                    waits=gw_, mark=False)
                r.op("T", lambda t, qlap=qlap: t.matmul(
                    pb[6][:, :], sgwt_h[:, :], qlap, start=False, stop=False),
                    mark=False)
                t_g[hl] = r.op("T", lambda t, qhap=qhap: t.matmul(
                    pb[6][:, :], sgwt_l[:, :], qhap, start=False, stop=True))
                if hl == 0:
                    tok = pend_av()
                    kv_free[pend_sl] = tok
                    pend_av = None
                    last_unit = tok
                t_sg = r.op("S", lambda s: s.activation(
                    gate_sb[:, :], pb[6][:, :], AF.Sigmoid, bias=sgb[:, :],
                    scale=1.0), waits=[t_g[hl], c_gb, t_a1_prev])
                gate_bank_free = t_sg
                t_ra0 = r.op("S", lambda s_, hl=hl: s_.copy(
                    rsum[:, :], pb[5][64 * hl:64 * hl + 1, :]),
                    waits=[last_unit, t_bc_prev])
                t_ra = r.op("V", lambda v, hl=hl: v.tensor_tensor(
                    rsum[:, :], rsum[:, :],
                    pb[5][64 * hl + 32:64 * hl + 33, :], AT.add),
                    waits=[t_ra0])
                t_rc = r.op("V", lambda v: v.reciprocal(
                    rsum[:, :], rsum[:, :]), waits=[t_ra])
                sums_free[0] = t_rc
                t_bc = r.op("T", lambda t: t.matmul(
                    pb[7][:, :], sones_f[:1, :], rsum[:1, :], start=True, stop=True),
                    waits=[t_rc, t_ones_f, bcast_bank_free])
                t_bc_prev = t_bc
                t_a1 = r.op("V", lambda v, hl=hl: v.tensor_tensor(
                    ao32[:, :], pb[3 + hl][:, :], gate_sb[:, :], AT.mult),
                    waits=[t_sg, last_unit, ao_buf_free])
                av_free[hl] = t_a1
                t_a1_prev = t_a1
                t_a2 = r.op("V", lambda v: v.tensor_tensor(
                    ao32[:, :], ao32[:, :], pb[7][:, :], AT.mult),
                    waits=[t_a1, t_bc])
                bcast_bank_free = t_a2
                t_h2 = r.op("S", lambda s: s.copy(aoh[:, :], ao32[:, :]),
                            waits=[t_a2])
                t_l2 = r.op("V", lambda v: v.tensor_tensor(
                    aol[:, :], ao32[:, :], aoh[:, :], AT.subtract), waits=[t_h2])
                dst = ci // 2
                ao_t = ao_inA if ci % 2 == 0 else ao_inB
                off_h = dst * AO2 + 0 * 256 * 512 + (hl * 128) * 512
                off_l = dst * AO2 + 1 * 256 * 512 + (hl * 128) * 512
                s_h = r.dma("G", "qG", dap(ao_t, off_h, [[512, 128], [1, 512]]),
                            aoh[:, :], waits=[t_h2])
                s_l = r.dma("G", "qG", dap(ao_t, off_l, [[512, 128], [1, 512]]),
                            aol[:, :], waits=[t_l2])
                ao_last = s_l
                ao_buf_free = [s_h, s_l]
            q_free[qp] = t_g[1]
            if cit == 7:
                t_a2a_aoA = r.op("G", lambda g: g.collective_compute(
                    "AllToAll", AT.bypass, replica_groups=[list(range(NCORES))],
                    ins=[ao_inA.ap().opt()], outs=[ao_outA.ap().opt()]),
                    waits=r.qbar("qG"), ctr="cc")

        t_a2a_aoB = r.op("G", lambda g: g.collective_compute(
            "AllToAll", AT.bypass, replica_groups=[list(range(NCORES))],
            ins=[ao_inB.ap().opt()], outs=[ao_outB.ap().opt()]),
            waits=r.qbar("qG"), ctr="cc")

        def ow_load(eg, first_wait, owraw, end_wait=None):
            toks = {}
            last = None
            for dch in range(2):
                lr = r.dma("Y", "qS",
                           dap(owraw, 0, [[NDC * 256, 128], [512, 8], [1, 512]]),
                           dap(wqkvo, 3 * D * D + (dch * 8 * 128) * D + eg * 512,
                               [[D, 128], [128 * D, 8], [1, 512]]),
                           waits=(first_wait if dch == 0 else [end_wait]) + [last])
                oh_ap = owh[:, dch * 4096:(dch + 1) * 4096]
                ol_ap = owl[:, dch * 4096:(dch + 1) * 4096]
                or_ap = owraw[:, :]
                th_ = r.op("S", lambda s_, oh_ap=oh_ap, or_ap=or_ap:
                           s_.copy(oh_ap, or_ap), waits=[lr])
                last = r.op("V", lambda v, ol_ap=ol_ap, or_ap=or_ap, oh_ap=oh_ap:
                            v.tensor_tensor(ol_ap, or_ap, oh_ap, AT.subtract),
                            waits=[th_])
                toks[dch] = last
            return toks
        owraw_B = SB("owraw_B", [128, NDC * 256], stk=stB)
        ow0_last = ow_load(0, [], owraw_B)

        r.emit_block(nc, sems)
        stB.close()
        stAB.close()
        stC = ExitStack()

        ach = [SB(f"ach{i}", [128, NDC * 128], BF16, stk=stC) for i in range(2)]
        acl = [SB(f"acl{i}", [128, NDC * 128], BF16, stk=stC) for i in range(2)]
        xsl = [SB(f"xsl{i}", [128, 512], stk=stC) for i in range(2)]
        owraw_C = SB("owraw_C", [128, NDC * 256], stk=stC)
        bnst2 = SB("bnst2", [128, 24], stk=stC)
        bnag2 = SB("bnag2", [128, 2], stk=stC)
        ln2t = SB("ln2t", [128, 1], stk=stC); ln2t2 = SB("ln2t2", [128, 1], stk=stC)
        ln2r = SB("ln2r", [128, 1], stk=stC)
        xn2 = SB("xn2", [128, D], stk=stC)

        x2buf = SB("bigA", [128, 8 * D], stk=stC)
        bank3 = {0: s_free[0], 1: s_free[1], 2: s_free[2], 3: av_free[0]}
        tp2 = {4: av_free[1], 5: sums_free[0]}
        ow_prev_mm = None
        ach_free = [None, None]
        xsl_free = [None, None]
        x2_toks = {}
        xn2T_last = None
        x2sp_st = []
        for tchalf in range(2):
          for eg in range(4):
            if tchalf == 0 and eg == 0:
                ow_lds = ow0_last
            else:
                ow_lds = ow_load(eg, [ow_mid_tok], owraw_C, end_wait=ow_prev_mm)
            for tc in range(tchalf * 4, tchalf * 4 + 4):
                bk = tc % 4
                asl = tc % 2
                ao_src = ao_outA if tc < 4 else ao_outB
                ao_tok = t_a2a_aoA if tc < 4 else t_a2a_aoB
                alds = []
                for part, buf in ((0, ach[asl]), (1, acl[asl])):
                    for rowhalf in range(2):
                        alds.append(r.dma(
                            "Y", "qS",
                            dap(buf, rowhalf * 128, [[NDC * 128, 128], [256, 8], [1, 128]]),
                            dap(ao_src, part * 256 * 512 + rowhalf * 128 * 512
                                + (tc % 4) * 128,
                                [[512, 128], [AO2, 8], [1, 128]]),
                            waits=[ao_tok, ach_free[asl]]))
                mml = None
                for dc in range(NDC):
                    mmw = []
                    if dc == 0:
                        mmw = [alds, ow_lds[0], bank3.get(bk)]
                    elif dc == 8:
                        mmw = [ow_lds[1]]
                    oh = owh[:, dc * 512:(dc + 1) * 512]
                    ol = owl[:, dc * 512:(dc + 1) * 512]
                    ah_ = ach[asl][:, dc * 128:(dc + 1) * 128]
                    al_ = acl[asl][:, dc * 128:(dc + 1) * 128]
                    r.op("T", lambda t, ah_=ah_, bk=bk, oh=oh, dc=dc: t.matmul(
                        pb[bk][:, :], ah_, oh, start=(dc == 0),
                        stop=False), waits=mmw, mark=False)
                    r.op("T", lambda t, ah_=ah_, bk=bk, ol=ol: t.matmul(
                        pb[bk][:, :], ah_, ol, start=False, stop=False),
                        mark=False)
                    mml = r.op("T", lambda t, al_=al_, bk=bk, oh=oh, dc=dc: t.matmul(
                        pb[bk][:, :], al_, oh, start=False,
                        stop=(dc == NDC - 1)))
                    if dc == 7:
                        ow_mid_tok = mml
                ach_free[asl] = mml
                xs = tc % 2
                xld = r.dma("Y", "qS", xsl[xs][:, :],
                            dap(x_in, tc * 128 * D + eg * 512, [[D, 128], [1, 512]]),
                            waits=[xsl_free[xs]])
                t_x2 = r.op("V", lambda v, bk=bk, xs=xs, tc=tc, eg=eg: v.tensor_tensor(
                    x2buf[:, tc * D + eg * 512:tc * D + eg * 512 + 512],
                    pb[bk][:, :], xsl[xs][:, :], AT.add), waits=[mml, xld])
                bank3[bk] = t_x2
                xsl_free[xs] = t_x2
                x2_toks[(tc, eg)] = t_x2
            ow_prev_mm = mml

        for tc in range(8):
            w0 = [x2_toks[(tc, eg)] for eg in range(4)]
            sp = r.dma("G", "qG", dap(x2_sp, tc * 128 * D, [[D, 128], [1, D]]),
                       x2buf[:, tc * D:(tc + 1) * D], waits=w0)
            x2sp_st.append(sp)
            xb = x2buf
            k = ln_chain(r, (None, lambda b, tc=tc, xb=xb:
                             xb[:, tc * D + b * 512:tc * D + (b + 1) * 512]),
                         bnst2, bnag2, ln2t, ln2t2, ln2r, w0[-1])
            for t0 in w0:
                pass
            t_xn2 = r.op("V", lambda v, tc=tc: v.tensor_scalar(
                xn2[:, :], x2buf[:, tc * D:(tc + 1) * D], bnag2[:, 0:1],
                ln2r[:, :], AT.subtract, AT.mult), waits=[k] + w0)
            xn2T_last = transpose_split(r, xn2, xnT_h, xnT_l, tc * 128, t_xn2,
                                        (4, 5), tp2)

        r.emit_block(nc, sems)
        stC.close()
        stBC.close()
        stD = ExitStack()

        for _eng, _nopfn in (("T", lambda t: t.nop()), ("V", lambda v: v.engine_nop()),
                             ("S", lambda s_: s_.nop()), ("G", lambda g: g.engine_nop()),
                             ("Y", lambda y: y.nop())):
            r.op(_eng, _nopfn, waits=list(x2sp_st), mark=False)
        sds_row = SB("sds_row", [1, D], stk=stD)
        c_dsr = r.dma("Y", "qS", sds_row[:, :], ds_in.ap())
        sds_b = SB("sds_b", [128, 512], stk=stD)
        t_dsb_cur = None
        dsb_hg = [-1]

        def ensure_dsb(hg):
            nonlocal t_dsb_cur
            if dsb_hg[0] == hg:
                return t_dsb_cur
            mmw = [c_dsr, t_ones_f, bank_f.get(4)]
            if t_dsb_cur is not None:
                mmw.append(t_dsb_cur)
            mm = r.op("T", lambda t, hg=hg: t.matmul(
                pb[4][:, :], sones_f[:1, :],
                sds_row[:1, hg * 512:(hg + 1) * 512], start=True, stop=True),
                waits=mmw)
            t_dsb_cur = r.op("V", lambda v: v.tensor_copy(sds_b[:, :], pb[4][:, :]),
                             waits=[mm])
            bank_f[4] = t_dsb_cur
            dsb_hg[0] = hg
            return t_dsb_cur

        fuse_h = SB("fuse_h", [128, NIC * 512], BF16, stk=stD)
        fuse_l = SB("fuse_l", [128, NIC * 512], BF16, stk=stD)
        gwb = [SB(f"gwb{i}", [128, NDC * 256], F8, stk=stD) for i in range(2)]
        uwb = [SB(f"uwb{i}", [128, NDC * 256], F8, stk=stD) for i in range(1)]
        dwb = [SB(f"dwb{i}", [128, 512], F8, stk=stD) for i in range(2)]
        fsil = SB("fsil", [128, 512], stk=stD)
        fup = SB("fup", [128, 512], stk=stD)
        x2t = [SB(f"x2t{i}", [128, 512], stk=stD) for i in range(2)]
        otile = [fsil, fup]

        out_last = None
        gw_free = [None, None]
        uw_free = None
        dw_free = [None, None]
        fs_free = None
        fu_free = None
        bank_f = {0: bank3[0], 1: bank3[1], 2: bank3[2], 3: bank3[3],
                  4: tp2[4], 5: tp2[5],
                  6: gate_bank_free, 7: bcast_bank_free}
        x2t_free = [None, None]
        ot_free = [None, None]
        prev_down = None
        for grp in range(2):
            fuse_toks = {}
            for ic in range(NIC):
                bkG = 4 + (ic % 2) * 2
                bkU = bkG + 1
                icg = ic // 2
                gsl = icg % 2
                if ic % 2 == 0:
                    rank = ic // 6
                    coli = ((icg * 2) % 6) * 128
                    g1 = r.dma("Y", "qS",
                               dap(gwb[gsl], 0, [[NDC * 256, 128], [256, NDC], [1, 256]]),
                               dap(pf_out, rank * PF_N + 0 * D * 768
                                   + coli, [[768, 128], [128 * 768, NDC], [1, 256]]),
                               waits=[gw_free[gsl], t_agf])
                    g2 = r.dma("Y", "qS",
                               dap(uwb[0], 0, [[NDC * 256, 128], [256, NDC], [1, 256]]),
                               dap(pf_out, rank * PF_N + 1 * D * 768
                                   + coli, [[768, 128], [128 * 768, NDC], [1, 256]]),
                               waits=[uw_free])
                    gw_lds = (g1, g2)
                mmlG = mmlU = None
                for dc in range(NDC):
                    mmw = [gw_lds[0], xn2T_last]
                    if dc == 0:
                        mmw += [bank_f.get(bkG)]
                    gq = gwb[gsl][:, dc * 256 + (ic % 2) * 128:dc * 256 + (ic % 2) * 128 + 128]
                    xh = xnT_h[:, dc * TPC + grp * 512:dc * TPC + grp * 512 + 512]
                    xl = xnT_l[:, dc * TPC + grp * 512:dc * TPC + grp * 512 + 512]
                    r.op("T", lambda t, gq=gq, bkG=bkG, xh=xh, dc=dc: t.matmul(
                        pb[bkG][:, :], gq, xh, start=(dc == 0),
                        stop=False), waits=mmw, mark=False)
                    mmlG = r.op("T", lambda t, gq=gq, bkG=bkG, xl=xl, dc=dc: t.matmul(
                        pb[bkG][:, :], gq, xl, start=False,
                        stop=(dc == NDC - 1)))
                for dc in range(NDC):
                    mmw = [gw_lds[1], xn2T_last]
                    if dc == 0:
                        mmw += [bank_f.get(bkU)]
                    uq = uwb[0][:, dc * 256 + (ic % 2) * 128:dc * 256 + (ic % 2) * 128 + 128]
                    xh = xnT_h[:, dc * TPC + grp * 512:dc * TPC + grp * 512 + 512]
                    xl = xnT_l[:, dc * TPC + grp * 512:dc * TPC + grp * 512 + 512]
                    r.op("T", lambda t, uq=uq, bkU=bkU, xh=xh, dc=dc: t.matmul(
                        pb[bkU][:, :], uq, xh, start=(dc == 0),
                        stop=False), waits=mmw, mark=False)
                    mmlU = r.op("T", lambda t, uq=uq, bkU=bkU, xl=xl, dc=dc: t.matmul(
                        pb[bkU][:, :], uq, xl, start=False,
                        stop=(dc == NDC - 1)))
                if ic % 2 == 1:
                    gw_free[gsl] = mmlG
                    uw_free = mmlU
                t_si = r.op("S", lambda s, bkG=bkG, ic=ic: s.activation(
                    fsil[:, :], pb[bkG][:, :], AF.Silu, scale=sgs[:, ic:ic + 1]),
                    waits=[mmlG, c_gs, fs_free]
                    + (list(ot_free) if ic == 0 else []))
                bank_f[bkG] = t_si
                t_up = r.op("V", lambda v, bkU=bkU, ic=ic: v.tensor_scalar(
                    fup[:, :], pb[bkU][:, :], sus[:, ic:ic + 1], None, AT.mult),
                    waits=[mmlU, c_us, fu_free]
                    + (list(ot_free) if ic == 0 else []))
                bank_f[bkU] = t_up
                t_f = r.op("V", lambda v: v.tensor_tensor(
                    fup[:, :], fup[:, :], fsil[:, :], AT.mult), waits=[t_si, t_up])
                fs_free = t_f
                t_fh = r.op("S", lambda s, ic=ic: s.copy(
                    fuse_h[:, ic * 512:(ic + 1) * 512], fup[:, :]),
                    waits=[t_f, prev_down if ic == 0 else None])
                t_fl = r.op("V", lambda v, ic=ic: v.tensor_tensor(
                    fuse_l[:, ic * 512:(ic + 1) * 512], fup[:, :],
                    fuse_h[:, ic * 512:(ic + 1) * 512], AT.subtract),
                    waits=[t_fh, prev_down if ic == 0 else None])
                fu_free = t_fl
                fuse_toks[ic] = t_fl
            for hg in range(4):
                mml = {tc: None for tc in range(4)}
                for ic in range(NIC):
                    sl = ic % 2
                    off = (2 * hg) * PF_N + SEC_DN + (ic * 128) * 256
                    dld = r.dma("Y", "qS",
                                dap(dwb[sl], 0, [[512, 128], [256, 2], [1, 256]]),
                                dap(pf_out, off, [[256, 128], [PF_N, 2], [1, 256]]),
                                waits=[dw_free[sl], t_agf])
                    for tc in range(4):
                        bk = tc
                        mmw = [dld, fuse_toks[ic]]
                        if ic == 0:
                            mmw.append(bank_f.get(bk))
                        fh = fuse_h[:, ic * 512 + tc * 128:ic * 512 + tc * 128 + 128]
                        fl = fuse_l[:, ic * 512 + tc * 128:ic * 512 + tc * 128 + 128]
                        r.op("T", lambda t, bk=bk, fh=fh, sl=sl, ic=ic: t.matmul(
                            pb[bk][:, :], fh, dwb[sl][:, :], start=(ic == 0),
                            stop=False), waits=mmw, mark=False)
                        mml[tc] = r.op("T", lambda t, bk=bk, fl=fl, sl=sl, ic=ic:
                                       t.matmul(pb[bk][:, :], fl, dwb[sl][:, :],
                                                start=False, stop=(ic == NIC - 1)))
                    dw_free[sl] = mml[3]
                prev_down = mml[3]
                t_dsb_now = ensure_dsb(hg)
                for tc in range(4):
                    xs = tc % 2
                    xld = r.dma("Y", "qS", x2t[xs][:, :],
                                dap(x2_sp, (grp * 512 + tc * 128) * D + hg * 512,
                                    [[D, 128], [1, 512]]),
                                waits=[x2t_free[xs], x2sp_st[grp * 4 + tc]])
                    ots = tc % 2
                    t_d1 = r.op("V", lambda v, tc=tc, ots=ots: v.tensor_tensor(
                        otile[ots][:, :], pb[tc][:, :],
                        sds_b[:, :], AT.mult),
                        waits=[mml[tc], t_dsb_now, ot_free[ots]])
                    bank_f[tc] = t_d1
                    t_d2 = r.op("V", lambda v, xs=xs, ots=ots: v.tensor_tensor(
                        otile[ots][:, :], otile[ots][:, :], x2t[xs][:, :], AT.add),
                        waits=[t_d1, xld])
                    x2t_free[xs] = t_d2
                    so = r.dma("G", "qG",
                               dap(out_t, (grp * 512 + tc * 128) * D + hg * 512,
                                   [[D, 128], [1, 512]]),
                               otile[ots][:, :], waits=[t_d2])
                    ot_free[ots] = so
                    out_last = so

        for tap_t, src_t in dbg_taps:
            out_last = r.dma("G", "qG", tap_t.ap(), src_t.ap(), waits=[out_last])

        r.op("G", lambda g: g.engine_nop(), waits=[out_last], mark=False)

        r.emit_block(nc, sems)
        stD.close()

    return nc


def prepare_in_maps(inputs):
    x = np.ascontiguousarray(np.asarray(inputs["L0_attn_x"], dtype=np.float32)).reshape(S, D)
    qw = np.asarray(inputs["L0_attn_q_w"], np.float32)
    kw = np.asarray(inputs["L0_attn_k_w"], np.float32)
    vw = np.asarray(inputs["L0_attn_v_w"], np.float32)
    ow = np.asarray(inputs["L0_attn_o_w"], np.float32)
    gwt = np.ascontiguousarray(np.asarray(inputs["L0_attn_gate_w"], np.float32).T)
    gb = np.ascontiguousarray(np.asarray(inputs["L0_attn_gate_b"], np.float32).reshape(HD, 1))
    gw_f = np.asarray(inputs["L0_ffn_gate_w"], np.float32)
    uw_f = np.asarray(inputs["L0_ffn_up_w"], np.float32)
    dw_f = np.asarray(inputs["L0_ffn_down_w"], np.float32)
    gs = np.ascontiguousarray(np.asarray(inputs["L0_ffn_gate_scale"], np.float32).reshape(NIC, 128).T)
    us = np.ascontiguousarray(np.asarray(inputs["L0_ffn_up_scale"], np.float32).reshape(NIC, 128).T)
    ds = np.ascontiguousarray(np.asarray(inputs["L0_ffn_down_scale"], np.float32).reshape(1, D))
    ident = np.eye(128, dtype=np.float32)
    k_idx = np.arange(128).reshape(1, 128, 1)
    q_idx = np.arange(512).reshape(1, 1, 512)
    o_idx = np.arange(4).reshape(4, 1, 1)
    masks = ((128 * o_idx + k_idx) <= q_idx).astype(np.float32)

    qkvoT = np.stack([np.ascontiguousarray(w.T) for w in (qw, kw, vw, ow)])
    in_maps = []
    for c in range(NCORES):
        in_maps.append({
            "x": np.ascontiguousarray(x[c * TPC:(c + 1) * TPC]),
            "wqkvo": qkvoT,
            "gu": np.ascontiguousarray(
                np.stack([gw_f[:, c * 768:(c + 1) * 768], uw_f[:, c * 768:(c + 1) * 768]])),
            "dn": np.ascontiguousarray(dw_f[:, c * 256:(c + 1) * 256]),
            "gwt": gwt, "gb": gb, "gs": gs, "us": us, "ds": ds,
            "ident": ident, "masks": masks,
        })
    return in_maps


def kernel(**inputs):
    in_maps = prepare_in_maps(inputs)
    nc = build_program(DBG)
    res = run_bass_kernel_spmd(nc, in_maps, core_ids=list(range(NCORES)))
    out = np.concatenate([np.asarray(r["out"]) for r in res.results], axis=0)
    return out.reshape(1, S, D).astype(np.float32)
